# version 30
# baseline (speedup 1.0000x reference)
import numpy as np
import ml_dtypes
from contextlib import ExitStack

import concourse.bass as bass
import concourse.mybir as mybir
from concourse.bass_utils import run_bass_kernel_spmd

F32 = mybir.dt.float32
F32R = mybir.dt.float32r
BF16 = mybir.dt.bfloat16
FP8 = mybir.dt.float8e4
AF = mybir.ActivationFunctionType
ALU = mybir.AluOpType
AX = mybir.AxisListType

TOK = 2048
D = 2048
NH = 16
UH = 8192
NG = 4
GT = TOK // NG
NT = TOK // 128
NST = GT // 8
SP = 3


class Ctr:
    __slots__ = ("h", "n", "eng", "ordered")

    def __init__(self, h, eng, ordered=False):
        self.h = h
        self.n = 0
        self.eng = eng
        self.ordered = ordered


class Res:
    __slots__ = ("writer", "readers")

    def __init__(self):
        self.writer = None
        self.readers = []


class P:
    ENGS = ("sync", "tensor", "vector", "scalar", "gpsimd")

    def __init__(self):
        self.ops = {e: [] for e in self.ENGS}
        self.seen = {e: {} for e in self.ENGS}
        self.all_waits = []
        self.own = {}

    def wait(self, eng, ctr, val):
        if ctr is None or val is None or val <= 0:
            return
        if ctr.eng == eng and ctr.ordered and eng in ("tensor", "sync"):
            return
        w = self.seen[eng]
        if w.get(ctr, 0) >= val:
            return
        w[ctr] = val
        self.all_waits.append((ctr, val))
        self.ops[eng].append(lambda e, _h=ctr.h, _v=val: e.wait_ge(_h, _v))

    def emit(self, eng, make, inc=None, waits=()):
        if eng in ("vector", "scalar"):
            ctr0 = self.own[eng]
            self.wait(eng, ctr0, ctr0.n)
            if inc is None:
                inc = (ctr0, 1)
        for wv in waits:
            self.wait(eng, wv[0], wv[1])
        if inc is not None:
            ctr, k = inc
            ctr.n += k

            def f(e, _m=make, _h=ctr.h, _k=k):
                _m(e).then_inc(_h, _k)

            self.ops[eng].append(f)
            return ctr.n
        self.ops[eng].append(lambda e, _m=make: _m(e))
        return None

    @staticmethod
    def w_waits(res):
        ws = list(res.readers)
        if res.writer is not None:
            ws.append(res.writer)
        return ws

    @staticmethod
    def note_write(res, ctr):
        res.writer = (ctr, ctr.n)
        res.readers = []

    @staticmethod
    def r_waits(res):
        return [res.writer] if res.writer is not None else []

    @staticmethod
    def note_read(res, ctr):
        res.readers.append((ctr, ctr.n))


def build_nc():
    nc = bass.Bass("TRN2", target_bir_lowering=False)

    x_in = nc.dram_tensor("x", [TOK, D], F32, kind="ExternalInput").ap()
    c_in = nc.dram_tensor("cvec", [1, D], F32, kind="ExternalInput").ap()
    w_in = [nc.dram_tensor(n, [D, D], F32, kind="ExternalInput").ap()
            for n in ("wq", "wk", "wv")]
    wg_in = nc.dram_tensor("wg", [2, D, UH], F32, kind="ExternalInput").ap()
    wl_in = nc.dram_tensor("wl", [UH, D], F32, kind="ExternalInput").ap()
    wmod_in = nc.dram_tensor("wmod", [D, 6 * D], F32, kind="ExternalInput").ap()
    id16_in = nc.dram_tensor("id16", [128, 128], BF16, kind="ExternalInput").ap()
    id32_in = nc.dram_tensor("id32", [128, 128], F32, kind="ExternalInput").ap()
    mask_in = nc.dram_tensor("mask", [128, 512], F32, kind="ExternalInput").ap()
    y_out = nc.dram_tensor("y", [TOK, D], F32, kind="ExternalOutput").ap()

    wqT3 = [nc.dram_tensor(f"wqT{w}", [4, 128, 16, 512], BF16, kind="Internal").ap()
            for w in range(3)]
    wqb3 = [nc.dram_tensor(f"wqb{w}", [D, D], BF16, kind="Internal").ap()
            for w in range(3)]
    wgq = nc.dram_tensor("wgq", [2, D, UH], FP8, kind="Internal").ap()
    wlq = nc.dram_tensor("wlq", [UH, D], FP8, kind="Internal").ap()
    h2d = nc.dram_tensor("h2d", [128, NG, 16, GT], FP8, kind="Internal").ap()
    x1d = nc.dram_tensor("x1d", [TOK, D], F32, kind="Internal").ap()

    p = P()
    es = ExitStack()

    def mksem(name, eng, ordered=False):
        return Ctr(es.enter_context(nc.semaphore(name)), eng, ordered)

    sCONST = mksem("sCONST", "sync")
    sX = [mksem(f"sX{i}", "sync") for i in range(2)]
    sWCOL = [mksem(f"sWCOL{i}", "sync") for i in range(4)]
    sWROW = [mksem(f"sWROW{i}", "sync") for i in range(4)]
    sWQT = [mksem(f"sWQT{i}", "sync") for i in range(2)]
    sWG = [mksem(f"sWG{i}", "sync") for i in range(2)]
    sWL = [mksem(f"sWL{i}", "sync") for i in range(2)]
    sH2L = [mksem(f"sH2L{i}", "sync") for i in range(2)]
    sX1L = [mksem(f"sX1L{i}", "sync") for i in range(2)]
    sX1S = [mksem(f"sX1S{i}", "sync") for i in range(2)]
    sH2S = mksem("sH2S", "sync")
    sOUT = [mksem(f"sOUT{i}", "sync") for i in range(2)]
    sSYNC = mksem("sSYNC", "sync", ordered=True)
    sWMOD = [mksem(f"sWMOD{i}", "gpsimd") for i in range(3)]
    sFFNQ = mksem("sFFNQ", "gpsimd")
    sQKVC = [mksem(f"sQKVC{i}", "gpsimd") for i in range(3)]
    sPE = mksem("sPE", "tensor", ordered=True)
    sDVE = mksem("sDVE", "vector")
    sACT = mksem("sACT", "scalar")
    p.own["vector"] = sDVE
    p.own["scalar"] = sACT

    FB = [es.enter_context(nc.psum_tensor(f"psF{i}", [128, 512], F32))
          for i in range(6)]
    BB = [es.enter_context(nc.psum_tensor(f"psB{i}", [128, 1024], BF16))
          for i in range(2)]
    fbank = [Res() for _ in range(6)]
    bbank = [Res() for _ in range(2)]

    def sb(name, shape, dt):
        return es.enter_context(nc.sbuf_tensor(name, shape, dt))

    id16 = sb("id16s", [128, 128], BF16)
    id32 = sb("id32s", [128, 128], F32)
    mask_sb = sb("masks", [128, 512], F32)
    c_sb = sb("c_sb", [128, 16], F32)
    c2_r = sb("c2_r", [128, 16, 2], F32R)
    mod_fm = sb("mod_fm", [128, 96], F32)
    ca1 = sb("ca1", [128, 16], F32)
    cb1 = sb("cb1", [128, 16], F32)
    ga_c = sb("ga_c", [128, 16], F32)
    gb_c = sb("gb_c", [128, 16], F32)
    tmpA = sb("tmpA", [128, 16], F32)
    tmpB = sb("tmpB", [128, 16], F32)
    tmpC = sb("tmpC", [128, 16], F32)
    tmpD = sb("tmpD", [128, 16], F32)
    ga_til = sb("ga_til", [128, 16, 16], F32)
    ones16 = sb("ones16", [128, 16], F32)
    stats = sb("stats", [128, 2, 24], F32)
    mv = sb("mv", [128, 2, 2], F32)
    negm = sb("negm", [128, 2], F32)
    veps = sb("veps", [128, 2], F32)
    stdv = sb("stdv", [128, 2], F32)
    rstd = sb("rstd", [128, 2], F32)
    x_t = sb("x_t", [128, 2, D], F32)
    norm_t = sb("norm_t", [128, D], BF16)
    x1_t = sb("x1_t", [128, D], F32)

    x_t_r = [Res(), Res()]
    norm_r = [Res(), Res()]
    ln_r = [Res(), Res()]
    x1_t_r = Res()

    p.emit("sync", lambda e: e.dma_start(out=id16[:, :], in_=id16_in[:, :]),
           inc=(sCONST, 16))
    p.emit("sync", lambda e: e.dma_start(out=id32[:, :], in_=id32_in[:, :]),
           inc=(sCONST, 16))
    p.emit("sync", lambda e: e.dma_start(out=mask_sb[:, :], in_=mask_in[:, :]),
           inc=(sCONST, 16))
    for fc in range(16):
        src = c_in[0:1, fc * 128:(fc + 1) * 128].rearrange("b p -> p b")
        p.emit("sync", lambda e, o=c_sb[:, fc:fc + 1], i=src: e.dma_start(out=o, in_=i),
               inc=(sCONST, 16))
    consts_ld = sCONST.n

    p.emit("scalar", lambda e: e.activation(out=c2_r[:, :, 0], in_=c_sb[:, :],
           func=AF.Silu), waits=[(sCONST, consts_ld)])
    silu_done = p.emit("scalar", lambda e: e.activation(out=c2_r[:, :, 1],
                       in_=c_sb[:, :], func=AF.Silu))

    h_scope = ExitStack()
    h_fm = h_scope.enter_context(nc.sbuf_tensor("h_fm", [128, 16, TOK], BF16))
    prep_scope = ExitStack()
    wtr = prep_scope.enter_context(nc.sbuf_tensor(
        "wtr", [128, 4, 2048], BF16, is_transpose_dest=True))
    wmod_sb = prep_scope.enter_context(
        nc.sbuf_tensor("wmod_sb", [128, 3, 2, 2048], F32R))
    modred = prep_scope.enter_context(nc.sbuf_tensor("modred", [128, 16, 2], F32))
    norm_h2 = prep_scope.enter_context(nc.sbuf_tensor("norm_h2", [128, D], BF16))
    wmod_r = [Res(), Res(), Res()]
    wtr_r = [Res(), Res(), Res(), Res()]

    p.emit("sync", lambda e: e.dma_start_transpose(out=wtr[:, 0, 0:128],
           in_=id16_in[:, :]), inc=(sWCOL[0], 16))
    wtr_r[0].writer = (sWCOL[0], sWCOL[0].n)

    def emit_wmod_load(jb2, dp):
        par = (jb2 * 8 + dp) % 3
        ws = P.w_waits(wmod_r[par])
        src = wmod_in[dp * 256:(dp + 1) * 256,
                      jb2 * 2048:(jb2 + 1) * 2048].rearrange(
            "(c p) j -> p c j", p=128)
        p.emit("gpsimd", lambda e, o=wmod_sb[:, par, :, :], i=src:
               e.dma_start(out=o, in_=i), inc=(sWMOD[par], 16), waits=ws)
        P.note_write(wmod_r[par], sWMOD[par])

    def emit_mod_block(jb2):
        bi = 4 + (jb2 % 2)
        bank = fbank[bi]
        bws = P.w_waits(bank) + [(sACT, silu_done)]
        for dp in range(8):
            par = (jb2 * 8 + dp) % 3
            ldv = wmod_r[par].writer
            for c in range(2):
                dc = dp * 2 + c
                for jl in range(16):
                    lhsT = wmod_sb[:, par, c, jl * 128:(jl + 1) * 128]
                    col = jl * 32 + dc * 2
                    p.emit("tensor",
                           lambda e, o=FB[bi][:, col:col + 2], l=lhsT,
                           r=c2_r[:, dc, :],
                           s=(dc == 0 and jl == 0), t=(dc == 15 and jl == 15):
                           e.matmul(o, l, r, start=s, stop=t,
                                    skip_group_check=True),
                           inc=((sPE, 1) if jl == 15 else None),
                           waits=((bws + [ldv]) if (c == 0 and jl == 0) else ()))
            P.note_read(wmod_r[par], sPE)
            if dp + 3 < 8:
                emit_wmod_load(jb2, dp + 3)
            elif jb2 + 1 < 6:
                emit_wmod_load(jb2 + 1, dp - 5)
        P.note_write(bank, sPE)
        srcv = FB[bi][:, :].rearrange("p (jl dc two) -> p jl two dc", jl=16, two=2)
        p.emit("vector", lambda e, o=modred[:, :, :], i_=srcv:
               e.tensor_reduce(out=o, in_=i_, axis=AX.X, op=ALU.add),
               waits=P.r_waits(bank))
        P.note_read(bank, sDVE)
        p.emit("vector", lambda e, o=mod_fm[:, jb2 * 16:(jb2 + 1) * 16],
               i_=modred[:, :, 0]: e.tensor_copy(out=o, in_=i_))

    def emit_modvec_a():
        p.emit("vector", lambda e: e.tensor_scalar(out=ca1[:, :], in0=mod_fm[:, 16:32],
               scalar1=1.0, scalar2=None, op0=ALU.add))
        return sDVE.n

    def emit_modvec_b():
        p.emit("vector", lambda e: e.tensor_scalar(out=cb1[:, :], in0=mod_fm[:, 64:80],
               scalar1=1.0, scalar2=None, op0=ALU.add))
        for (s0, dst) in ((32, ga_c), (80, gb_c)):
            p.emit("vector", lambda e, s=s0: e.tensor_scalar(out=tmpA[:, :],
                   in0=mod_fm[:, s:s + 16], scalar1=0.0, scalar2=None, op0=ALU.is_ge))
            p.emit("vector", lambda e: e.tensor_scalar(out=tmpB[:, :], in0=tmpA[:, :],
                   scalar1=2.0, scalar2=-1.0, op0=ALU.mult, op1=ALU.add))
            sv = p.emit("scalar", lambda e, s=s0: e.activation(out=tmpC[:, :],
                        in_=mod_fm[:, s:s + 16], func=AF.Abs),
                        waits=[(sDVE, sDVE.n)])
            p.emit("vector", lambda e: e.tensor_scalar(out=tmpD[:, :], in0=tmpC[:, :],
                   scalar1=0.25, scalar2=None, op0=ALU.max), waits=[(sACT, sv)])
            p.emit("vector", lambda e, d=dst: e.tensor_tensor(out=d[:, :],
                   in0=tmpB[:, :], in1=tmpD[:, :], op=ALU.mult))
        p.emit("vector", lambda e: e.memset(ones16[:, :], 1.0))
        for h in range(16):
            p.emit("vector", lambda e, h=h: e.tensor_single_scalar(
                   out=ga_til[:, h, :], in_=ones16[:, :], scalar=ga_c[:, h:h + 1],
                   op=ALU.mult))
        return sDVE.n

    def emit_ln_chain(ln_par, src_ap):
        for k in range(4):
            p.emit("vector", lambda e, k=k, s=src_ap: e.bn_stats(
                stats[:, ln_par, 6 * k:6 * k + 6], s[:, 512 * k:512 * (k + 1)]))
        p.emit("vector", lambda e: e.bn_aggr(mv[:, ln_par, :], stats[:, ln_par, :]))
        p.emit("vector", lambda e: e.tensor_scalar(out=negm[:, ln_par:ln_par + 1],
               in0=mv[:, ln_par, 0:1], scalar1=-1.0, scalar2=None, op0=ALU.mult))
        va = p.emit("vector", lambda e: e.tensor_scalar(out=veps[:, ln_par:ln_par + 1],
                    in0=mv[:, ln_par, 1:2], scalar1=1e-5, scalar2=None, op0=ALU.add))
        sq = p.emit("scalar", lambda e: e.activation(out=stdv[:, ln_par:ln_par + 1],
                    in_=veps[:, ln_par:ln_par + 1], func=AF.Sqrt), waits=[(sDVE, va)])
        p.emit("vector", lambda e: e.reciprocal(out=rstd[:, ln_par:ln_par + 1],
               in_=stdv[:, ln_par:ln_par + 1]), waits=[(sACT, sq)])

    def emit_norm(ln_par, src_ap, nb=0):
        dst = norm_t if nb == 0 else norm_h2
        ws = P.w_waits(norm_r[nb])
        p.emit("vector", lambda e, s=src_ap, d=dst: e.tensor_scalar(
            out=d[:, :], in0=s, scalar1=negm[:, ln_par:ln_par + 1],
            scalar2=rstd[:, ln_par:ln_par + 1], op0=ALU.add, op1=ALU.mult),
            waits=ws)
        P.note_write(norm_r[nb], sDVE)
        P.note_read(ln_r[ln_par], sDVE)

    modvec_done = None
    modvec_a_done = None

    h_tile_act = [None] * NT

    def emit_h_tile(i):
        par = i % 2
        ws = P.w_waits(x_t_r[par])
        ld = p.emit("sync", lambda e, o=x_t[:, par, :],
                    i_=x_in[i * 128:(i + 1) * 128, :]:
                    e.dma_start(out=o, in_=i_), inc=(sX[par], 16), waits=ws)
        P.note_write(x_t_r[par], sX[par])
        for wv in P.w_waits(ln_r[par]):
            p.wait("vector", wv[0], wv[1])
        p.wait("vector", sX[par], ld)
        emit_ln_chain(par, x_t[:, par, :])
        P.note_write(ln_r[par], sDVE)
        emit_norm(par, x_t[:, par, :], nb=par)
        P.note_read(x_t_r[par], sDVE)
        nval = sDVE.n
        nsrc = norm_t if par == 0 else norm_h2
        for fc in range(16):
            bb = bbank[fc % 2]
            ws2 = P.w_waits(bb) + [(sDVE, nval), (sCONST, consts_ld)]
            p.emit("tensor", lambda e, o=BB[fc % 2][:, 0:128],
                   i_=nsrc[:, fc * 128:(fc + 1) * 128]:
                   e.transpose(o, i_, id16[:, :]), inc=(sPE, 1), waits=ws2)
            P.note_write(bb, sPE)
            P.note_read(norm_r[par], sPE)
            wv2 = P.r_waits(bb) + [(sDVE, modvec_a_done)]
            p.emit("scalar", lambda e, fc=fc, o=h_fm[:, fc, i * 128:(i + 1) * 128],
                   i_=BB[fc % 2][:, 0:128]:
                   e.activation(out=o, in_=i_, func=AF.Identity,
                                scale=ca1[:, fc:fc + 1], bias=mod_fm[:, fc:fc + 1]),
                   waits=wv2)
            P.note_read(bb, sACT)
        h_tile_act[i] = sACT.n

    prep_done = [None, None, None]

    qkvc_done = [None, None, None]

    def emit_prep_chunk(w, fc):
        par = (w * 16 + fc) % 4
        ws = P.w_waits(wtr_r[par]) + [(sQKVC[w], qkvc_done[w])]
        tv = p.emit("sync", lambda e, o=wtr[:, par, :],
                    i_=wqb3[w][:, fc * 128:(fc + 1) * 128]:
                    e.dma_start_transpose(out=o, in_=i_), inc=(sWCOL[par], 16),
                    waits=ws)
        p.wait("sync", sWCOL[par], tv)
        dstv = wqT3[w][:, :, fc, :].rearrange("eb p e -> p eb e")
        srcv = wtr[:, par, :].rearrange("p (eb e) -> p eb e", eb=4)
        p.emit("sync", lambda e, o=dstv, i_=srcv: e.dma_start(out=o, in_=i_),
               inc=(sWROW[par], 16))
        wtr_r[par].writer = None
        wtr_r[par].readers = [(sWROW[par], sWROW[par].n)]
        if fc == 15:
            for s_ in sWROW:
                p.wait("sync", s_, s_.n)
            sSYNC.n += 1
            p.emit("sync", lambda e: e.sem_inc(sSYNC.h, 1))
            prep_done[w] = sSYNC.n

    emit_wmod_load(0, 0)
    emit_wmod_load(0, 1)
    emit_wmod_load(0, 2)
    for w in range(3):
        for q in range(4):
            p.emit("gpsimd", lambda e, o=wqb3[w][q * 512:(q + 1) * 512, :],
                   i=w_in[w][q * 512:(q + 1) * 512, :]: e.dma_start(out=o, in_=i),
                   inc=(sQKVC[w], 16))
        qkvc_done[w] = sQKVC[w].n
    emit_prep_chunk(0, 0)
    pi = 0
    for jb2 in range(2):
        emit_mod_block(jb2)
        for _ in range(8):
            emit_prep_chunk(pi // 16, pi % 16)
            pi += 1
    modvec_a_done = emit_modvec_a()
    for hi in range(NT):
        emit_h_tile(hi)
        for _ in range(2):
            if pi < 48:
                emit_prep_chunk(pi // 16, pi % 16)
                pi += 1
        if hi % 4 == 3:
            emit_mod_block(2 + hi // 4)
    h_trans_done = sPE.n
    modvec_done = emit_modvec_b()
    prep_scope.close()
    p3 = ExitStack()
    q_sb = p3.enter_context(nc.sbuf_tensor("q_sb", [128, NST, 8, NH], BF16))
    k_sb = p3.enter_context(nc.sbuf_tensor("k_sb", [128, NST, 8, NH], BF16))
    v_sb = p3.enter_context(nc.sbuf_tensor("v_sb", [128, NST, 8, NH], BF16))
    attn_sb = p3.enter_context(nc.sbuf_tensor("attn_sb", [128, NH, GT], BF16))
    wqT_sb = p3.enter_context(nc.sbuf_tensor("wqT_sb", [128, 2, 16, 512], BF16))
    pbf4 = p3.enter_context(nc.sbuf_tensor("pbf4", [128, 2, 512], BF16))
    pt4 = p3.enter_context(nc.sbuf_tensor("pt4", [128, 2, 512], BF16))
    vt4 = p3.enter_context(nc.sbuf_tensor("vt4", [128, 2, 512], BF16))
    negmx = p3.enter_context(nc.sbuf_tensor("negmx", [128, 2, 4], F32))
    sums4 = p3.enter_context(nc.sbuf_tensor("sums4", [128, 2, 4], F32))
    rsum4 = p3.enter_context(nc.sbuf_tensor("rsum4", [128, 2, 4], F32))
    h2st = p3.enter_context(nc.sbuf_tensor("h2st", [128, 16, GT], FP8))

    wqT_r = [Res(), Res()]

    def emit_wqT_load(g, w, eb):
        par = (g * 12 + w * 4 + eb) % 2
        ws = P.w_waits(wqT_r[par]) + [(sSYNC, prep_done[w]),
                                      (sPE, h_trans_done),
                                      (sDVE, modvec_done)]
        p.emit("sync", lambda e, o=wqT_sb[:, par, :, :], i=wqT3[w][eb]:
               e.dma_start(out=o, in_=i), inc=(sWQT[par], 16), waits=ws)
        P.note_write(wqT_r[par], sWQT[par])

    h_done_act = sACT.n
    cast_jobs = []
    for g_ in range(2):
        for fc_ in range(4):
            cast_jobs.append(lambda e, o=wgq[g_, fc_ * 512:(fc_ + 1) * 512, :],
                             i=wg_in[g_, fc_ * 512:(fc_ + 1) * 512, :]:
                             e.dma_start(out=o, in_=i))
    for ub_ in range(8):
        cast_jobs.append(lambda e, o=wlq[ub_ * 1024:(ub_ + 1) * 1024, :],
                         i=wl_in[ub_ * 1024:(ub_ + 1) * 1024, :]:
                         e.dma_start(out=o, in_=i))
    cast_next = [0]

    def emit_ffn_casts(n, gate):
        for _ in range(n):
            if cast_next[0] >= len(cast_jobs):
                return
            p.emit("gpsimd", cast_jobs[cast_next[0]], inc=(sFFNQ, 16),
                   waits=(gate if gate is not None else ()))
            gate = None
            cast_next[0] += 1

    wqT_r = [Res(), Res()]
    wqT_pre = {}
    qkv_r = Res()
    attn_tile_r = [Res() for _ in range(4)]
    sub_r = {n: [Res() for _ in range(2)] for n in
             ("sm", "pb", "pn", "pt", "vt", "nm", "su", "rs")}
    h2st_r = Res()

    for g in range(NG):
        gsl = slice(g * GT, (g + 1) * GT)
        for w in range(3):
            dst_sb = (q_sb, k_sb, v_sb)[w]
            for eb in range(4):
                par = (g * 12 + w * 4 + eb) % 2
                if (g, w, eb) in wqT_pre:
                    ldv = wqT_pre.pop((g, w, eb))
                else:
                    emit_wqT_load(g, w, eb)
                    ldv = (sWQT[par], sWQT[par].n)
                for ec in range(4):
                    head = eb * 4 + ec
                    bi = 4 + (head % 2)
                    bank = fbank[bi]
                    bws = P.w_waits(bank) + [ldv, (sACT, h_tile_act[g * 4 + 3])]
                    if w == 0 and eb == 0 and ec == 0:
                        bws += P.w_waits(qkv_r)
                    for fc in range(16):
                        lhsT = wqT_sb[:, par, fc, ec * 128:(ec + 1) * 128]
                        rhs = h_fm[:, fc, gsl]
                        p.emit("tensor", lambda e, o=FB[bi][:, :], l=lhsT, r=rhs,
                               s=(fc == 0), t=(fc == 15):
                               e.matmul(o, l, r, start=s, stop=t),
                               inc=((sPE, 1) if fc == 15 else None),
                               waits=(bws if fc == 0 else ()))
                    P.note_write(bank, sPE)
                    P.note_read(wqT_r[par], sPE)
                    dst = dst_sb[:, :, :, head]
                    srcv = FB[bi][:, :].rearrange("p (st t) -> p st t", t=8)
                    p.emit("scalar", lambda e, o=dst, i_=srcv:
                           e.activation(out=o, in_=i_, func=AF.Copy),
                           waits=P.r_waits(bank))
                    P.note_read(bank, sACT)
        qkv_done = sACT.n
        if g + 1 < NG:
            for eb in range(2):
                emit_wqT_load(g + 1, 0, eb)
                parn = ((g + 1) * 12 + eb) % 2
                wqT_pre[(g + 1, 0, eb)] = (sWQT[parn], sWQT[parn].n)
        emit_ffn_casts(4, [(sACT, qkv_done)])
        attn_asm_val = [0, 0, 0, 0]
        exp_val = {}

        def emit_s4(w):
            par = w % 2
            bank = fbank[par]
            ws = P.w_waits(bank) + [(sACT, qkv_done)]
            for k in range(4):
                st = 4 * w + k
                lhsT = q_sb[:, st, :, :].rearrange("p t h -> p (t h)")
                rhs = k_sb[:, st, :, :].rearrange("p t h -> p (t h)")
                p.emit("tensor", lambda e, o=FB[par][:, k * 128:(k + 1) * 128],
                       l=lhsT, r=rhs, s=(k == 0):
                       e.matmul(o, l, r, start=s, stop=True, skip_group_check=True),
                       inc=((sPE, 1) if k == 3 else None),
                       waits=(ws if k == 0 else ()))
            P.note_write(bank, sPE)
            P.note_read(qkv_r, sPE)

        def emit_sma4(w):
            par = w % 2
            bank = fbank[par]
            smb = fbank[4 + par]
            r_sm = sub_r["sm"][par]
            ws = P.w_waits(r_sm) + P.r_waits(bank) + P.w_waits(smb)
            p.emit("vector", lambda e, o=FB[4 + par][:, :], i0=FB[par][:, :]:
                   e.scalar_tensor_tensor(out=o, in0=i0, scalar=1.0,
                                          in1=mask_sb[:, :],
                                          op0=ALU.mult, op1=ALU.add), waits=ws)
            P.note_read(bank, sDVE)
            P.note_write(r_sm, sDVE)
            smb.writer = (sDVE, sDVE.n)
            smb.readers = []
            r_nm = sub_r["nm"][par]
            srcv = FB[4 + par][:, :].rearrange("p (s f) -> p s f", s=4)
            p.emit("vector", lambda e, o=negmx[:, par, :], i_=srcv:
                   e.reduce_max(out=o, in_=i_, axis=AX.X, negate=True),
                   waits=P.w_waits(r_nm))
            P.note_write(r_nm, sDVE)
            dv = sDVE.n
            r_pb = sub_r["pb"][par]
            r_su = sub_r["su"][par]
            ws2 = P.w_waits(r_pb) + P.w_waits(r_su) + [(sDVE, dv)]
            for k in range(4):
                p.emit("scalar", lambda e, k=k, o=pbf4[:, par, k * 128:(k + 1) * 128],
                       i_=FB[4 + par][:, k * 128:(k + 1) * 128],
                       b=negmx[:, par, k:k + 1], a=sums4[:, par, k:k + 1]:
                       e.activation(out=o, in_=i_, func=AF.Exp, bias=b, scale=1.0,
                                    accum_out=a), waits=(ws2 if k == 0 else ()))
            P.note_read(r_sm, sACT)
            P.note_read(r_nm, sACT)
            P.note_write(r_pb, sACT)
            P.note_write(r_su, sACT)
            P.note_read(smb, sACT)
            exp_val[w] = sACT.n

        def emit_rrs4(w):
            par = w % 2
            r_su = sub_r["su"][par]
            r_rs = sub_r["rs"][par]
            p.emit("vector", lambda e, o=rsum4[:, par, :], i_=sums4[:, par, :]:
                   e.reciprocal(out=o, in_=i_),
                   waits=P.w_waits(r_rs) + [(sACT, exp_val[w])])
            P.note_read(r_su, sDVE)
            P.note_write(r_rs, sDVE)
            r_pb = sub_r["pb"][par]
            for k in range(4):
                p.emit("vector", lambda e, k=k, o=pbf4[:, par, k * 128:(k + 1) * 128],
                       i_=pbf4[:, par, k * 128:(k + 1) * 128],
                       s=rsum4[:, par, k:k + 1]:
                       e.tensor_scalar(out=o, in0=i_, scalar1=s, scalar2=None,
                                       op0=ALU.mult))
            P.note_read(r_rs, sDVE)
            r_pn = sub_r["pn"][par]
            P.note_write(r_pn, sDVE)

        def emit_ptt4(w):
            par = w % 2
            bb = bbank[par]
            r_pn = sub_r["pn"][par]
            ws = P.w_waits(bb) + P.r_waits(r_pn)
            for k in range(4):
                p.emit("tensor", lambda e, k=k, o=BB[par][:, k * 128:(k + 1) * 128],
                       i_=pbf4[:, par, k * 128:(k + 1) * 128]:
                       e.transpose(o, i_, id16[:, :]), waits=(ws if k == 0 else ()))
            P.note_read(r_pn, sPE)
            P.note_read(sub_r["pb"][par], sPE)
            for k in range(4):
                st = 4 * w + k
                v8f = v_sb[:, st, :, :].rearrange("p t h -> p (t h)")
                p.emit("tensor", lambda e, k=k,
                       o=BB[par][:, 512 + k * 128:512 + (k + 1) * 128], i_=v8f:
                       e.transpose(o, i_, id16[:, :]),
                       inc=((sPE, 1) if k == 3 else None))
            P.note_write(bb, sPE)
            P.note_read(qkv_r, sPE)
            tv = sPE.n
            r_pt = sub_r["pt"][par]
            r_vt = sub_r["vt"][par]
            p.emit("scalar", lambda e, o=pt4[:, par, :], i_=BB[par][:, 0:512]:
                   e.activation(out=o, in_=i_, func=AF.Copy),
                   waits=P.w_waits(r_pt) + [(sPE, tv)])
            P.note_write(r_pt, sACT)
            p.emit("scalar", lambda e, o=vt4[:, par, :], i_=BB[par][:, 512:1024]:
                   e.activation(out=o, in_=i_, func=AF.Copy),
                   waits=P.w_waits(r_vt))
            P.note_write(r_vt, sACT)
            P.note_read(bb, sACT)

        def emit_av4(w):
            par = w % 2
            bank = fbank[2 + par]
            r_pt = sub_r["pt"][par]
            r_vt = sub_r["vt"][par]
            ws = P.w_waits(bank) + P.r_waits(r_pt) + P.r_waits(r_vt)
            for k in range(4):
                p.emit("tensor", lambda e, k=k,
                       o=FB[2 + par][:, k * 128:(k + 1) * 128],
                       l=vt4[:, par, k * 128:(k + 1) * 128],
                       r=pt4[:, par, k * 128:(k + 1) * 128], s=(k == 0):
                       e.matmul(o, l, r, start=s, stop=True, skip_group_check=True),
                       inc=((sPE, 1) if k == 3 else None),
                       waits=(ws if k == 0 else ()))
            P.note_write(bank, sPE)
            P.note_read(r_pt, sPE)
            P.note_read(r_vt, sPE)
            tt = (4 * w) // 16
            ws3 = P.r_waits(bank) + P.w_waits(attn_tile_r[tt]) + [(sDVE, modvec_done)]
            gat = ga_til[:, :, :].rearrange("p h (s t) -> p h s t", s=2)
            for j in range(2):
                st0 = 4 * w + 2 * j
                dst = attn_sb[:, :, st0 * 8:(st0 + 2) * 8].rearrange(
                    "p h (s t) -> p h s t", s=2)
                src0 = FB[2 + par][:, 2 * j * 128:(2 * j + 2) * 128].rearrange(
                    "p (s t h) -> p h s t", s=2, t=8)
                p.emit("vector", lambda e, o=dst, a=src0, g_=gat: e.tensor_tensor(
                       out=o, in0=a, in1=g_, op=ALU.mult),
                       waits=(ws3 if j == 0 else ()))
            P.note_read(bank, sDVE)
            attn_asm_val[tt] = sDVE.n

        def emit_res_tile(g, tt):
            gi = g * 4 + tt
            par = gi % 2
            ws = P.w_waits(x_t_r[par])
            ld = p.emit("sync", lambda e, o=x_t[:, par, :],
                        i_=x_in[gi * 128:(gi + 1) * 128, :]:
                        e.dma_start(out=o, in_=i_), inc=(sX[par], 16), waits=ws)
            P.note_write(x_t_r[par], sX[par])
            wx1 = P.w_waits(x1_t_r)
            for hq in range(4):
                bb = bbank[hq % 2]
                ws2 = P.w_waits(bb) + ([(sDVE, attn_asm_val[tt])] if hq == 0 else [])
                for k in range(4):
                    hc = hq * 4 + k
                    p.emit("tensor", lambda e, k=k,
                           o=BB[hq % 2][:, k * 128:(k + 1) * 128],
                           i_=attn_sb[:, hc, tt * 128:(tt + 1) * 128]:
                           e.transpose(o, i_, id16[:, :]),
                           inc=((sPE, 1) if k == 3 else None),
                           waits=(ws2 if k == 0 else ()))
                P.note_write(bb, sPE)
                wv3 = P.r_waits(bb) + [(sX[par], ld)] + (wx1 if hq == 0 else [])
                p.emit("vector", lambda e, hq=hq,
                       o=x1_t[:, hq * 512:(hq + 1) * 512],
                       i0=BB[hq % 2][:, 0:512],
                       i1=x_t[:, par, hq * 512:(hq + 1) * 512]:
                       e.tensor_tensor(out=o, in0=i0, in1=i1, op=ALU.add),
                       waits=wv3)
                P.note_read(bb, sDVE)
            attn_tile_r[tt].writer = None
            attn_tile_r[tt].readers = [(sPE, sPE.n)]
            P.note_read(x_t_r[par], sDVE)
            P.note_write(x1_t_r, sDVE)
            rv = sDVE.n
            p.emit("sync", lambda e, o=x1d[gi * 128:(gi + 1) * 128, :],
                   i_=x1_t[:, :]: e.dma_start(out=o, in_=i_),
                   inc=(sX1S[par], 16), waits=[(sDVE, rv)])
            P.note_read(x1_t_r, sX1S[par])
            for wv in P.w_waits(ln_r[par]):
                p.wait("vector", wv[0], wv[1])
            emit_ln_chain(par, x1_t[:, :])
            P.note_write(ln_r[par], sDVE)
            emit_norm(par, x1_t[:, :])
            P.note_read(x1_t_r, sDVE)
            nval = sDVE.n
            for fc in range(16):
                bb = bbank[fc % 2]
                ws2 = P.w_waits(bb) + [(sDVE, nval)]
                p.emit("tensor", lambda e, o=BB[fc % 2][:, 0:128],
                       i_=norm_t[:, fc * 128:(fc + 1) * 128]:
                       e.transpose(o, i_, id16[:, :]), inc=(sPE, 1), waits=ws2)
                P.note_write(bb, sPE)
                P.note_read(norm_r[0], sPE)
                wv2 = P.r_waits(bb) + \
                    (P.w_waits(h2st_r) if (fc == 0 and tt == 0) else [])
                p.emit("vector", lambda e, fc=fc,
                       o=h2st[:, fc, tt * 128:(tt + 1) * 128],
                       i_=BB[fc % 2][:, 0:128]: e.tensor_scalar(
                       out=o, in0=i_, scalar1=cb1[:, fc:fc + 1],
                       scalar2=mod_fm[:, 48 + fc:49 + fc],
                       op0=ALU.mult, op1=ALU.add), waits=wv2)
                P.note_read(bb, sDVE)

        NW = NST // 4
        for w in range(NW + 1):
            if w < NW:
                emit_s4(w)
                emit_sma4(w)
            if w >= 1:
                emit_rrs4(w - 1)
                emit_ptt4(w - 1)
                emit_av4(w - 1)
            if w >= 1 and ((w - 1) % 4) == 3:
                emit_res_tile(g, (w - 1) // 4)
        hv = sDVE.n
        p.emit("sync", lambda e, o=h2d[:, g, :, :], i_=h2st[:, :, :]:
               e.dma_start(out=o, in_=i_), inc=(sH2S, 16), waits=[(sDVE, hv)])
        h2st_r.writer = None
        h2st_r.readers = [(sH2S, sH2S.n)]
        qkv_r.writer = None
        qkv_r.readers = [(sDVE, sDVE.n), (sPE, sPE.n), (sACT, sACT.n)]
        emit_ffn_casts(2, [(sDVE, sDVE.n)])

    emit_ffn_casts(len(cast_jobs), None)
    ffn_cast_done = sFFNQ.n
    p.wait("sync", sX1S[0], sX1S[0].n)
    p.wait("sync", sX1S[1], sX1S[1].n)
    p.wait("sync", sH2S, sH2S.n)
    sSYNC.n += 1
    p.emit("sync", lambda e: e.sem_inc(sSYNC.h, 1))
    p3_done = sSYNC.n
    p3_end = dict(pe=sPE.n, dve=sDVE.n, act=sACT.n)
    p3.close()
    h_scope.close()

    p4 = ExitStack()
    h2_sb = p4.enter_context(nc.sbuf_tensor("h2_sb", [128, 2, 16, GT], FP8))
    act_sb = p4.enter_context(nc.sbuf_tensor("act_sb", [128, 64, GT], FP8))
    wg_sb = p4.enter_context(nc.sbuf_tensor("wg_sb", [128, 2, 2, 16, 512], FP8))
    wl_sb = p4.enter_context(nc.sbuf_tensor("wl_sb", [128, 2, 4, 512], FP8))
    gelu_sb = p4.enter_context(nc.sbuf_tensor("gelu_sb", [128, 2, GT], BF16))
    g1_sb = p4.enter_context(nc.sbuf_tensor("g1_sb", [128, 2, GT], BF16))
    ffn_fm = p4.enter_context(nc.sbuf_tensor("ffn_fm", [128, 16, GT], BF16))
    ffn_t = p4.enter_context(nc.sbuf_tensor("ffn_t", [128, 2, D], BF16))
    out_t = p4.enter_context(nc.sbuf_tensor("out_t", [128, 2, D], F32))

    h2_r = [Res(), Res()]
    act_r = Res()
    wg_r = [Res(), Res()]
    wl_r = [Res(), Res()]
    gelu_r = [Res(), Res()]
    g1_r = [Res(), Res()]
    ffn_fm_r = Res()
    ffn_t_r = [Res(), Res()]
    out_t_r = [Res(), Res()]

    p4_gate = [(sSYNC, p3_done), (sPE, p3_end["pe"]), (sDVE, p3_end["dve"]),
               (sACT, p3_end["act"]), (sFFNQ, ffn_cast_done)]

    def emit_h2_load(g):
        buf = g % 2
        ws = P.w_waits(h2_r[buf]) + (p4_gate if g == 0 else [])
        p.emit("sync", lambda e, o=h2_sb[:, buf, :, :], i_=h2d[:, g, :, :]:
               e.dma_start(out=o, in_=i_), inc=(sH2L[buf], 16), waits=ws)
        h2_r[buf].writer = (sH2L[buf], sH2L[buf].n)
        h2_r[buf].readers = []

    emit_h2_load(0)

    for g in range(NG):
        buf = g % 2
        h2v = h2_r[buf].writer
        for ub in range(16):
            par = ub % 2
            wsw = P.w_waits(wg_r[par])
            for gate in range(2):
                for fq in range(4):
                    src = wgq[gate, fq * 512:(fq + 1) * 512,
                              ub * 512:(ub + 1) * 512].rearrange(
                        "(fc p) u -> p fc u", p=128)
                    p.emit("sync", lambda e,
                           o=wg_sb[:, par, gate, fq * 4:(fq + 1) * 4, :], i=src:
                           e.dma_start(out=o, in_=i), inc=(sWG[par], 16),
                           waits=(wsw if (gate == 0 and fq == 0) else ()))
            P.note_write(wg_r[par], sWG[par])
            wgv = (sWG[par], sWG[par].n)
            for uc in range(4):
                ucp = uc % 2
                b0 = fbank[ucp]
                b1 = fbank[2 + ucp]
                uu = ub * 4 + uc
                DR = mybir.MatmulPerfMode.DoubleRow
                ws0 = P.w_waits(b0) + [wgv, h2v]
                for f2 in range(8):
                    lhsT = wg_sb[:, par, 0, 2 * f2:2 * f2 + 2, uc * 128:(uc + 1) * 128]
                    rhs = h2_sb[:, buf, 2 * f2:2 * f2 + 2, :]
                    p.emit("tensor", lambda e, o=FB[ucp][:, :], l=lhsT, r=rhs,
                           s=(f2 == 0), t=(f2 == 7):
                           e.matmul(o, l, r, start=s, stop=t, perf_mode=DR),
                           inc=((sPE, 1) if f2 == 7 else None),
                           waits=(ws0 if f2 == 0 else ()))
                P.note_write(b0, sPE)
                ws1 = P.w_waits(b1)
                for f2 in range(8):
                    lhsT = wg_sb[:, par, 1, 2 * f2:2 * f2 + 2, uc * 128:(uc + 1) * 128]
                    rhs = h2_sb[:, buf, 2 * f2:2 * f2 + 2, :]
                    p.emit("tensor", lambda e, o=FB[2 + ucp][:, :], l=lhsT, r=rhs,
                           s=(f2 == 0), t=(f2 == 7):
                           e.matmul(o, l, r, start=s, stop=t, perf_mode=DR),
                           inc=((sPE, 1) if f2 == 7 else None),
                           waits=(ws1 if f2 == 0 else ()))
                P.note_write(b1, sPE)
                P.note_read(wg_r[par], sPE)
                P.note_read(h2_r[buf], sPE)
                p.emit("scalar", lambda e, o=gelu_sb[:, ucp, :], i_=FB[ucp][:, :]:
                       e.activation(out=o, in_=i_, func=AF.Gelu_apprx_tanh),
                       waits=P.r_waits(b0) + P.w_waits(gelu_r[ucp]))
                P.note_read(b0, sACT)
                P.note_write(gelu_r[ucp], sACT)
                p.emit("scalar", lambda e, o=g1_sb[:, ucp, :], i_=FB[2 + ucp][:, :]:
                       e.activation(out=o, in_=i_, func=AF.Copy),
                       waits=P.r_waits(b1) + P.w_waits(g1_r[ucp]))
                P.note_read(b1, sACT)
                P.note_write(g1_r[ucp], sACT)
                ev = sACT.n
                wsx = [(sACT, ev)] + (P.w_waits(act_r) if (ub == 0 and uc == 0) else [])
                p.emit("vector", lambda e, o=act_sb[:, uu, :], a=gelu_sb[:, ucp, :],
                       b=g1_sb[:, ucp, :]:
                       e.tensor_tensor(out=o, in0=a, in1=b, op=ALU.mult), waits=wsx)
                P.note_read(gelu_r[ucp], sDVE)
                P.note_read(g1_r[ucp], sDVE)
        act_done = sDVE.n
        act_r.writer = (sDVE, act_done)
        act_r.readers = []
        if g + 1 < NG:
            emit_h2_load(g + 1)
        for eq in range(4):
            banks = [fbank[i] for i in range(4)]
            for ub in range(16):
                par = ub % 2
                wsl = P.w_waits(wl_r[par])
                src = wlq[ub * 512:(ub + 1) * 512,
                          eq * 512:(eq + 1) * 512].rearrange("(uc p) e -> p uc e",
                                                             p=128)
                p.emit("sync", lambda e, o=wl_sb[:, par, :, :], i=src:
                       e.dma_start(out=o, in_=i), inc=(sWL[par], 16), waits=wsl)
                P.note_write(wl_r[par], sWL[par])
                wlv = (sWL[par], sWL[par].n)
                DR = mybir.MatmulPerfMode.DoubleRow
                for u2 in range(2):
                    uu = ub * 4 + 2 * u2
                    for ec in range(4):
                        bank = banks[ec]
                        ws2 = [wlv]
                        if ub == 0 and u2 == 0:
                            ws2 += P.w_waits(bank) + [(sDVE, act_done)]
                        lhsT = wl_sb[:, par, 2 * u2:2 * u2 + 2,
                                     ec * 128:(ec + 1) * 128]
                        rhs = act_sb[:, uu:uu + 2, :]
                        last = (u2 == 1 and ec == 3)
                        p.emit("tensor", lambda e, o=FB[ec][:, :], l=lhsT, r=rhs,
                               s=(ub == 0 and u2 == 0), t=(ub == 15 and u2 == 1):
                               e.matmul(o, l, r, start=s, stop=t, perf_mode=DR),
                               inc=((sPE, 1) if last else None), waits=ws2)
                P.note_read(wl_r[par], sPE)
            for ec in range(4):
                P.note_write(banks[ec], sPE)
            P.note_read(act_r, sPE)
            for ec in range(4):
                ecg = eq * 4 + ec
                ws3 = P.r_waits(banks[ec]) + \
                    (P.w_waits(ffn_fm_r) if (eq == 0 and ec == 0) else [])
                p.emit("scalar", lambda e, ecg=ecg, o=ffn_fm[:, ecg, :],
                       i_=FB[ec][:, :], s=gb_c[:, ecg:ecg + 1]:
                       e.activation(out=o, in_=i_, func=AF.Identity, scale=s),
                       waits=ws3)
                P.note_read(banks[ec], sACT)
        ffn_ev = sACT.n
        ffn_fm_r.writer = (sACT, ffn_ev)
        ffn_fm_r.readers = []
        for tt in range(4):
            gi = g * 4 + tt
            par = gi % 2
            ld = p.emit("sync", lambda e, o=x1_t[:, :],
                        i_=x1d[gi * 128:(gi + 1) * 128, :]:
                        e.dma_start(out=o, in_=i_), inc=(sX1L[par], 16),
                        waits=P.w_waits(x1_t_r))
            P.note_write(x1_t_r, sX1L[par])
            wf = P.w_waits(ffn_t_r[par])
            for hc in range(16):
                bb = bbank[hc % 2]
                ws2 = P.w_waits(bb) + ([(sACT, ffn_ev)] if hc == 0 else [])
                p.emit("tensor", lambda e, o=BB[hc % 2][:, 0:128],
                       i_=ffn_fm[:, hc, tt * 128:(tt + 1) * 128]:
                       e.transpose(o, i_, id16[:, :]), inc=(sPE, 1), waits=ws2)
                P.note_write(bb, sPE)
                p.emit("scalar", lambda e, hc=hc,
                       o=ffn_t[:, par, hc * 128:(hc + 1) * 128],
                       i_=BB[hc % 2][:, 0:128]:
                       e.activation(out=o, in_=i_, func=AF.Copy),
                       waits=P.r_waits(bb) + (wf if hc == 0 else []))
                P.note_read(bb, sACT)
            P.note_read(ffn_fm_r, sPE)
            ev = sACT.n
            ffn_t_r[par].writer = (sACT, ev)
            ffn_t_r[par].readers = []
            wsr = P.w_waits(out_t_r[par]) + [(sX1L[par], ld), (sACT, ev)]
            p.emit("vector", lambda e, o=out_t[:, par, :], a=x1_t[:, :],
                   b=ffn_t[:, par, :]:
                   e.tensor_tensor(out=o, in0=a, in1=b, op=ALU.add), waits=wsr)
            P.note_read(x1_t_r, sDVE)
            P.note_read(ffn_t_r[par], sDVE)
            P.note_write(out_t_r[par], sDVE)
            rv = sDVE.n
            p.emit("sync", lambda e, o=y_out[gi * 128:(gi + 1) * 128, :],
                   i_=out_t[:, par, :]: e.dma_start(out=o, in_=i_),
                   inc=(sOUT[par], 16), waits=[(sDVE, rv)])
            P.note_read(out_t_r[par], sOUT[par])

    p.ops["sync"].append(lambda e, _h=sOUT[0].h, _v=sOUT[0].n: e.wait_ge(_h, _v))
    p.ops["sync"].append(lambda e, _h=sOUT[1].h, _v=sOUT[1].n: e.wait_ge(_h, _v))

    p4.close()

    for (ctr, val) in p.all_waits:
        assert val <= ctr.n, f"wait beyond final count: {val} > {ctr.n}"

    with nc.Block() as blk:
        @blk.sync
        def _(e):
            for f in p.ops["sync"]:
                f(e)

        @blk.gpsimd
        def _(e):
            for f in p.ops["gpsimd"]:
                f(e)

        @blk.tensor
        def _(e):
            for f in p.ops["tensor"]:
                f(e)

        @blk.vector
        def _(e):
            for f in p.ops["vector"]:
                f(e)

        @blk.scalar
        def _(e):
            for f in p.ops["scalar"]:
                f(e)

    es.close()
    return nc


def _make_consts():
    bf = ml_dtypes.bfloat16
    id16 = np.eye(128, dtype=np.float32).astype(bf)
    id32 = np.eye(128, dtype=np.float32)
    mask = np.full((128, 128), -1e30, dtype=np.float32)
    for t in range(8):
        blk = np.where(np.tril(np.ones((16, 16), dtype=bool)), 0.0, -1e30)
        mask[t * 16:(t + 1) * 16, t * 16:(t + 1) * 16] = blk
    mask4 = np.ascontiguousarray(np.tile(mask, (1, 4)).astype(np.float32))
    return id16, id32, mask4


def kernel(L0_attn_input_seq_embs, L0_attn_Wq, L0_attn_Wk, L0_attn_Wv,
           L0_ffn_w_gating, L0_ffn_w_linear, L0_c, L0_wmod):
    x = np.ascontiguousarray(np.asarray(L0_attn_input_seq_embs, dtype=np.float32))
    wq = np.ascontiguousarray(np.asarray(L0_attn_Wq, dtype=np.float32))
    wk = np.ascontiguousarray(np.asarray(L0_attn_Wk, dtype=np.float32))
    wv = np.ascontiguousarray(np.asarray(L0_attn_Wv, dtype=np.float32))
    wg = np.ascontiguousarray(np.asarray(L0_ffn_w_gating, dtype=np.float32))
    wl = np.ascontiguousarray(np.asarray(L0_ffn_w_linear, dtype=np.float32))
    c = np.ascontiguousarray(np.asarray(L0_c, dtype=np.float32))
    wmod = np.ascontiguousarray(np.asarray(L0_wmod, dtype=np.float32))

    id16, id32, mask = _make_consts()
    nc = build_nc()

    in_maps = []
    for core in range(8):
        b = core // 4
        s = core % 4
        in_maps.append({
            "x": np.ascontiguousarray(x[b, s * TOK:(s + 1) * TOK, :]),
            "cvec": np.ascontiguousarray(c[b:b + 1, :]),
            "wq": wq, "wk": wk, "wv": wv,
            "wg": wg, "wl": wl, "wmod": wmod,
            "id16": id16, "id32": id32, "mask": mask,
        })
    res = run_bass_kernel_spmd(nc, in_maps, core_ids=list(range(8)))
    y = np.empty((2, 8192, D), dtype=np.float32)
    for core in range(8):
        b = core // 4
        s = core % 4
        y[b, s * TOK:(s + 1) * TOK, :] = res.results[core]["y"]
    return y


# version 31 (final)
# speedup vs baseline: 1.0489x; 1.0489x over previous
import numpy as np
import ml_dtypes
from contextlib import ExitStack

import concourse.bass as bass
import concourse.mybir as mybir
from concourse.bass_utils import run_bass_kernel_spmd

F32 = mybir.dt.float32
F32R = mybir.dt.float32r
BF16 = mybir.dt.bfloat16
FP8 = mybir.dt.float8e4
AF = mybir.ActivationFunctionType
ALU = mybir.AluOpType
AX = mybir.AxisListType

TOK = 2048
D = 2048
NH = 16
UH = 8192
NG = 4
GT = TOK // NG
NT = TOK // 128
NST = GT // 8
SP = 3


class Ctr:
    __slots__ = ("h", "n", "eng", "ordered")

    def __init__(self, h, eng, ordered=False):
        self.h = h
        self.n = 0
        self.eng = eng
        self.ordered = ordered


class Res:
    __slots__ = ("writer", "readers")

    def __init__(self):
        self.writer = None
        self.readers = []


class P:
    ENGS = ("sync", "tensor", "vector", "scalar", "gpsimd")

    def __init__(self):
        self.ops = {e: [] for e in self.ENGS}
        self.seen = {e: {} for e in self.ENGS}
        self.all_waits = []
        self.own = {}

    def wait(self, eng, ctr, val):
        if ctr is None or val is None or val <= 0:
            return
        if ctr.eng == eng and ctr.ordered and eng in ("tensor", "sync"):
            return
        w = self.seen[eng]
        if w.get(ctr, 0) >= val:
            return
        w[ctr] = val
        self.all_waits.append((ctr, val))
        self.ops[eng].append(lambda e, _h=ctr.h, _v=val: e.wait_ge(_h, _v))

    def emit(self, eng, make, inc=None, waits=()):
        if eng in ("vector", "scalar"):
            ctr0 = self.own[eng]
            self.wait(eng, ctr0, ctr0.n)
            if inc is None:
                inc = (ctr0, 1)
        for wv in waits:
            self.wait(eng, wv[0], wv[1])
        if inc is not None:
            ctr, k = inc
            ctr.n += k

            def f(e, _m=make, _h=ctr.h, _k=k):
                _m(e).then_inc(_h, _k)

            self.ops[eng].append(f)
            return ctr.n
        self.ops[eng].append(lambda e, _m=make: _m(e))
        return None

    @staticmethod
    def w_waits(res):
        ws = list(res.readers)
        if res.writer is not None:
            ws.append(res.writer)
        return ws

    @staticmethod
    def note_write(res, ctr):
        res.writer = (ctr, ctr.n)
        res.readers = []

    @staticmethod
    def r_waits(res):
        return [res.writer] if res.writer is not None else []

    @staticmethod
    def note_read(res, ctr):
        res.readers.append((ctr, ctr.n))


def build_nc():
    nc = bass.Bass("TRN2", target_bir_lowering=False)

    x_in = nc.dram_tensor("x", [TOK, D], F32, kind="ExternalInput").ap()
    c_in = nc.dram_tensor("cvec", [1, D], F32, kind="ExternalInput").ap()
    w_in = [nc.dram_tensor(n, [D, D], F32, kind="ExternalInput").ap()
            for n in ("wq", "wk", "wv")]
    wg_in = nc.dram_tensor("wg", [2, D, UH], F32, kind="ExternalInput").ap()
    wl_in = nc.dram_tensor("wl", [UH, D], F32, kind="ExternalInput").ap()
    wmod_in = nc.dram_tensor("wmod", [D, 6 * D], F32, kind="ExternalInput").ap()
    id16_in = nc.dram_tensor("id16", [128, 128], BF16, kind="ExternalInput").ap()
    id32_in = nc.dram_tensor("id32", [128, 128], F32, kind="ExternalInput").ap()
    mask_in = nc.dram_tensor("mask", [128, 512], F32, kind="ExternalInput").ap()
    y_out = nc.dram_tensor("y", [TOK, D], F32, kind="ExternalOutput").ap()

    wqT3 = [nc.dram_tensor(f"wqT{w}", [4, 128, 16, 512], BF16, kind="Internal").ap()
            for w in range(3)]
    wqb3 = [nc.dram_tensor(f"wqb{w}", [D, D], BF16, kind="Internal").ap()
            for w in range(3)]
    wgq = nc.dram_tensor("wgq", [2, D, UH], FP8, kind="Internal").ap()
    wlq = nc.dram_tensor("wlq", [UH, D], FP8, kind="Internal").ap()
    h2d = nc.dram_tensor("h2d", [128, NG, 16, GT], FP8, kind="Internal").ap()
    x1d = nc.dram_tensor("x1d", [TOK, D], F32, kind="Internal").ap()

    p = P()
    es = ExitStack()

    def mksem(name, eng, ordered=False):
        return Ctr(es.enter_context(nc.semaphore(name)), eng, ordered)

    sCONST = mksem("sCONST", "sync")
    sX = [mksem(f"sX{i}", "sync") for i in range(2)]
    sWCOL = [mksem(f"sWCOL{i}", "sync") for i in range(4)]
    sWROW = [mksem(f"sWROW{i}", "sync") for i in range(4)]
    sWQT = [mksem(f"sWQT{i}", "sync") for i in range(2)]
    sWG = [mksem(f"sWG{i}", "sync") for i in range(2)]
    sWL = [mksem(f"sWL{i}", "sync") for i in range(2)]
    sH2L = [mksem(f"sH2L{i}", "sync") for i in range(2)]
    sX1L = [mksem(f"sX1L{i}", "sync") for i in range(2)]
    sX1S = [mksem(f"sX1S{i}", "sync") for i in range(2)]
    sH2S = mksem("sH2S", "sync")
    sOUT = [mksem(f"sOUT{i}", "sync") for i in range(2)]
    sSYNC = mksem("sSYNC", "sync", ordered=True)
    sWMOD = [mksem(f"sWMOD{i}", "gpsimd") for i in range(2)]
    sFFNQ = mksem("sFFNQ", "gpsimd")
    sQKVC = [mksem(f"sQKVC{i}", "gpsimd") for i in range(3)]
    sPE = mksem("sPE", "tensor", ordered=True)
    sDVE = mksem("sDVE", "vector")
    sACT = mksem("sACT", "scalar")
    p.own["vector"] = sDVE
    p.own["scalar"] = sACT

    FB = [es.enter_context(nc.psum_tensor(f"psF{i}", [128, 512], F32))
          for i in range(6)]
    BB = [es.enter_context(nc.psum_tensor(f"psB{i}", [128, 1024], BF16))
          for i in range(2)]
    fbank = [Res() for _ in range(6)]
    bbank = [Res() for _ in range(2)]

    def sb(name, shape, dt):
        return es.enter_context(nc.sbuf_tensor(name, shape, dt))

    id16 = sb("id16s", [128, 128], BF16)
    id32 = sb("id32s", [128, 128], F32)
    mask_sb = sb("masks", [128, 512], F32)
    c_sb = sb("c_sb", [128, 16], F32)
    c2_r = sb("c2_r", [128, 16, 2], F32R)
    mod_fm = sb("mod_fm", [128, 96], F32)
    ca1 = sb("ca1", [128, 16], F32)
    cb1 = sb("cb1", [128, 16], F32)
    ga_c = sb("ga_c", [128, 16], F32)
    gb_c = sb("gb_c", [128, 16], F32)
    tmpA = sb("tmpA", [128, 16], F32)
    tmpB = sb("tmpB", [128, 16], F32)
    tmpC = sb("tmpC", [128, 16], F32)
    tmpD = sb("tmpD", [128, 16], F32)
    ga_til = sb("ga_til", [128, 16, 16], F32)
    ones16 = sb("ones16", [128, 16], F32)
    stats = sb("stats", [128, 2, 24], F32)
    mv = sb("mv", [128, 2, 2], F32)
    negm = sb("negm", [128, 2], F32)
    veps = sb("veps", [128, 2], F32)
    stdv = sb("stdv", [128, 2], F32)
    rstd = sb("rstd", [128, 2], F32)
    x_t = sb("x_t", [128, 2, D], F32)
    norm_t = sb("norm_t", [128, D], BF16)
    x1_t = sb("x1_t", [128, D], F32)

    x_t_r = [Res(), Res()]
    norm_r = [Res(), Res()]
    ln_r = [Res(), Res()]
    x1_t_r = Res()

    p.emit("sync", lambda e: e.dma_start(out=id16[:, :], in_=id16_in[:, :]),
           inc=(sCONST, 16))
    p.emit("sync", lambda e: e.dma_start(out=id32[:, :], in_=id32_in[:, :]),
           inc=(sCONST, 16))
    p.emit("sync", lambda e: e.dma_start(out=mask_sb[:, :], in_=mask_in[:, :]),
           inc=(sCONST, 16))
    for fc in range(16):
        src = c_in[0:1, fc * 128:(fc + 1) * 128].rearrange("b p -> p b")
        p.emit("sync", lambda e, o=c_sb[:, fc:fc + 1], i=src: e.dma_start(out=o, in_=i),
               inc=(sCONST, 16))
    consts_ld = sCONST.n

    p.emit("scalar", lambda e: e.activation(out=c2_r[:, :, 0], in_=c_sb[:, :],
           func=AF.Silu), waits=[(sCONST, consts_ld)])
    silu_done = p.emit("scalar", lambda e: e.activation(out=c2_r[:, :, 1],
                       in_=c_sb[:, :], func=AF.Silu))

    h_scope = ExitStack()
    h_fm = h_scope.enter_context(nc.sbuf_tensor("h_fm", [128, 16, TOK], BF16))
    prep_scope = ExitStack()
    wtr = prep_scope.enter_context(nc.sbuf_tensor(
        "wtr", [128, 4, 2048], BF16, is_transpose_dest=True))
    wmod_sb = prep_scope.enter_context(
        nc.sbuf_tensor("wmod_sb", [128, 2, 2, 2048], F32R))
    modred = prep_scope.enter_context(nc.sbuf_tensor("modred", [128, 16, 2], F32))
    norm_h2 = prep_scope.enter_context(nc.sbuf_tensor("norm_h2", [128, D], BF16))
    wmod_r = [Res(), Res()]
    wtr_r = [Res(), Res(), Res(), Res()]

    p.emit("sync", lambda e: e.dma_start_transpose(out=wtr[:, 0, 0:128],
           in_=id16_in[:, :]), inc=(sWCOL[0], 16))
    wtr_r[0].writer = (sWCOL[0], sWCOL[0].n)

    def emit_wmod_load(jb2, dp):
        par = (jb2 * 8 + dp) % 2
        ws = P.w_waits(wmod_r[par])
        src = wmod_in[dp * 256:(dp + 1) * 256,
                      jb2 * 2048:(jb2 + 1) * 2048].rearrange(
            "(c p) j -> p c j", p=128)
        p.emit("gpsimd", lambda e, o=wmod_sb[:, par, :, :], i=src:
               e.dma_start(out=o, in_=i), inc=(sWMOD[par], 16), waits=ws)
        P.note_write(wmod_r[par], sWMOD[par])

    def emit_mod_block(jb2):
        bi = 4 + (jb2 % 2)
        bank = fbank[bi]
        bws = P.w_waits(bank) + [(sACT, silu_done)]
        for dp in range(8):
            par = (jb2 * 8 + dp) % 2
            ldv = wmod_r[par].writer
            for c in range(2):
                dc = dp * 2 + c
                for jl in range(16):
                    lhsT = wmod_sb[:, par, c, jl * 128:(jl + 1) * 128]
                    col = jl * 32 + dc * 2
                    p.emit("tensor",
                           lambda e, o=FB[bi][:, col:col + 2], l=lhsT,
                           r=c2_r[:, dc, :],
                           s=(dc == 0 and jl == 0), t=(dc == 15 and jl == 15):
                           e.matmul(o, l, r, start=s, stop=t,
                                    skip_group_check=True),
                           inc=((sPE, 1) if jl == 15 else None),
                           waits=((bws + [ldv]) if (c == 0 and jl == 0) else ()))
            P.note_read(wmod_r[par], sPE)
            if dp + 2 < 8:
                emit_wmod_load(jb2, dp + 2)
            elif jb2 + 1 < 6:
                emit_wmod_load(jb2 + 1, dp - 6)
        P.note_write(bank, sPE)
        srcv = FB[bi][:, :].rearrange("p (jl dc two) -> p jl two dc", jl=16, two=2)
        p.emit("vector", lambda e, o=modred[:, :, :], i_=srcv:
               e.tensor_reduce(out=o, in_=i_, axis=AX.X, op=ALU.add),
               waits=P.r_waits(bank))
        P.note_read(bank, sDVE)
        p.emit("vector", lambda e, o=mod_fm[:, jb2 * 16:(jb2 + 1) * 16],
               i_=modred[:, :, 0]: e.tensor_copy(out=o, in_=i_))

    def emit_modvec_a():
        p.emit("vector", lambda e: e.tensor_scalar(out=ca1[:, :], in0=mod_fm[:, 16:32],
               scalar1=1.0, scalar2=None, op0=ALU.add))
        return sDVE.n

    def emit_modvec_b():
        p.emit("vector", lambda e: e.tensor_scalar(out=cb1[:, :], in0=mod_fm[:, 64:80],
               scalar1=1.0, scalar2=None, op0=ALU.add))
        for (s0, dst) in ((32, ga_c), (80, gb_c)):
            p.emit("vector", lambda e, s=s0: e.tensor_scalar(out=tmpA[:, :],
                   in0=mod_fm[:, s:s + 16], scalar1=0.0, scalar2=None, op0=ALU.is_ge))
            p.emit("vector", lambda e: e.tensor_scalar(out=tmpB[:, :], in0=tmpA[:, :],
                   scalar1=2.0, scalar2=-1.0, op0=ALU.mult, op1=ALU.add))
            sv = p.emit("scalar", lambda e, s=s0: e.activation(out=tmpC[:, :],
                        in_=mod_fm[:, s:s + 16], func=AF.Abs),
                        waits=[(sDVE, sDVE.n)])
            p.emit("vector", lambda e: e.tensor_scalar(out=tmpD[:, :], in0=tmpC[:, :],
                   scalar1=0.25, scalar2=None, op0=ALU.max), waits=[(sACT, sv)])
            p.emit("vector", lambda e, d=dst: e.tensor_tensor(out=d[:, :],
                   in0=tmpB[:, :], in1=tmpD[:, :], op=ALU.mult))
        p.emit("vector", lambda e: e.memset(ones16[:, :], 1.0))
        for h in range(16):
            p.emit("vector", lambda e, h=h: e.tensor_single_scalar(
                   out=ga_til[:, h, :], in_=ones16[:, :], scalar=ga_c[:, h:h + 1],
                   op=ALU.mult))
        return sDVE.n

    def emit_ln_chain(ln_par, src_ap):
        for k in range(4):
            p.emit("vector", lambda e, k=k, s=src_ap: e.bn_stats(
                stats[:, ln_par, 6 * k:6 * k + 6], s[:, 512 * k:512 * (k + 1)]))
        p.emit("vector", lambda e: e.bn_aggr(mv[:, ln_par, :], stats[:, ln_par, :]))
        p.emit("vector", lambda e: e.tensor_scalar(out=negm[:, ln_par:ln_par + 1],
               in0=mv[:, ln_par, 0:1], scalar1=-1.0, scalar2=None, op0=ALU.mult))
        va = p.emit("vector", lambda e: e.tensor_scalar(out=veps[:, ln_par:ln_par + 1],
                    in0=mv[:, ln_par, 1:2], scalar1=1e-5, scalar2=None, op0=ALU.add))
        sq = p.emit("scalar", lambda e: e.activation(out=stdv[:, ln_par:ln_par + 1],
                    in_=veps[:, ln_par:ln_par + 1], func=AF.Sqrt), waits=[(sDVE, va)])
        p.emit("vector", lambda e: e.reciprocal(out=rstd[:, ln_par:ln_par + 1],
               in_=stdv[:, ln_par:ln_par + 1]), waits=[(sACT, sq)])

    def emit_norm(ln_par, src_ap, nb=0):
        dst = norm_t if nb == 0 else norm_h2
        ws = P.w_waits(norm_r[nb])
        p.emit("vector", lambda e, s=src_ap, d=dst: e.tensor_scalar(
            out=d[:, :], in0=s, scalar1=negm[:, ln_par:ln_par + 1],
            scalar2=rstd[:, ln_par:ln_par + 1], op0=ALU.add, op1=ALU.mult),
            waits=ws)
        P.note_write(norm_r[nb], sDVE)
        P.note_read(ln_r[ln_par], sDVE)

    modvec_done = None
    modvec_a_done = None

    h_tile_act = [None] * NT

    def emit_h_tile(i):
        par = i % 2
        ws = P.w_waits(x_t_r[par])
        ld = p.emit("sync", lambda e, o=x_t[:, par, :],
                    i_=x_in[i * 128:(i + 1) * 128, :]:
                    e.dma_start(out=o, in_=i_), inc=(sX[par], 16), waits=ws)
        P.note_write(x_t_r[par], sX[par])
        for wv in P.w_waits(ln_r[par]):
            p.wait("vector", wv[0], wv[1])
        p.wait("vector", sX[par], ld)
        emit_ln_chain(par, x_t[:, par, :])
        P.note_write(ln_r[par], sDVE)
        emit_norm(par, x_t[:, par, :], nb=par)
        P.note_read(x_t_r[par], sDVE)
        nval = sDVE.n
        nsrc = norm_t if par == 0 else norm_h2
        for fc in range(16):
            bb = bbank[fc % 2]
            ws2 = P.w_waits(bb) + [(sDVE, nval), (sCONST, consts_ld)]
            p.emit("tensor", lambda e, o=BB[fc % 2][:, 0:128],
                   i_=nsrc[:, fc * 128:(fc + 1) * 128]:
                   e.transpose(o, i_, id16[:, :]), inc=(sPE, 1), waits=ws2)
            P.note_write(bb, sPE)
            P.note_read(norm_r[par], sPE)
            wv2 = P.r_waits(bb) + [(sDVE, modvec_a_done)]
            p.emit("scalar", lambda e, fc=fc, o=h_fm[:, fc, i * 128:(i + 1) * 128],
                   i_=BB[fc % 2][:, 0:128]:
                   e.activation(out=o, in_=i_, func=AF.Identity,
                                scale=ca1[:, fc:fc + 1], bias=mod_fm[:, fc:fc + 1]),
                   waits=wv2)
            P.note_read(bb, sACT)
        h_tile_act[i] = sACT.n

    prep_done = [None, None, None]

    qkvc_done = [None, None, None]

    def emit_prep_chunk(w, fc):
        par = (w * 16 + fc) % 4
        ws = P.w_waits(wtr_r[par]) + [(sQKVC[w], qkvc_done[w])]
        tv = p.emit("sync", lambda e, o=wtr[:, par, :],
                    i_=wqb3[w][:, fc * 128:(fc + 1) * 128]:
                    e.dma_start_transpose(out=o, in_=i_), inc=(sWCOL[par], 16),
                    waits=ws)
        p.wait("sync", sWCOL[par], tv)
        dstv = wqT3[w][:, :, fc, :].rearrange("eb p e -> p eb e")
        srcv = wtr[:, par, :].rearrange("p (eb e) -> p eb e", eb=4)
        p.emit("sync", lambda e, o=dstv, i_=srcv: e.dma_start(out=o, in_=i_),
               inc=(sWROW[par], 16))
        wtr_r[par].writer = None
        wtr_r[par].readers = [(sWROW[par], sWROW[par].n)]
        if fc == 15:
            for s_ in sWROW:
                p.wait("sync", s_, s_.n)
            sSYNC.n += 1
            p.emit("sync", lambda e: e.sem_inc(sSYNC.h, 1))
            prep_done[w] = sSYNC.n

    emit_wmod_load(0, 0)
    emit_wmod_load(0, 1)
    for w in range(3):
        for q in range(4):
            p.emit("gpsimd", lambda e, o=wqb3[w][q * 512:(q + 1) * 512, :],
                   i=w_in[w][q * 512:(q + 1) * 512, :]: e.dma_start(out=o, in_=i),
                   inc=(sQKVC[w], 16))
        qkvc_done[w] = sQKVC[w].n
    emit_prep_chunk(0, 0)
    pi = 0
    for jb2 in range(2):
        emit_mod_block(jb2)
        for _ in range(8):
            emit_prep_chunk(pi // 16, pi % 16)
            pi += 1
    modvec_a_done = emit_modvec_a()
    for hi in range(NT):
        emit_h_tile(hi)
        for _ in range(2):
            if pi < 48:
                emit_prep_chunk(pi // 16, pi % 16)
                pi += 1
        if hi % 4 == 3:
            emit_mod_block(2 + hi // 4)
    h_trans_done = sPE.n
    modvec_done = emit_modvec_b()
    prep_scope.close()
    p3 = ExitStack()
    q_sb = p3.enter_context(nc.sbuf_tensor("q_sb", [128, NST, 8, NH], BF16))
    k_sb = p3.enter_context(nc.sbuf_tensor("k_sb", [128, NST, 8, NH], BF16))
    v_sb = p3.enter_context(nc.sbuf_tensor("v_sb", [128, NST, 8, NH], BF16))
    attn_sb = p3.enter_context(nc.sbuf_tensor("attn_sb", [128, NH, GT], BF16))
    wqT_sb = p3.enter_context(nc.sbuf_tensor("wqT_sb", [128, 2, 16, 512], BF16))
    pbf4 = p3.enter_context(nc.sbuf_tensor("pbf4", [128, 2, 512], BF16))
    pt4 = p3.enter_context(nc.sbuf_tensor("pt4", [128, 2, 512], BF16))
    vt4 = p3.enter_context(nc.sbuf_tensor("vt4", [128, 2, 512], BF16))
    negmx = p3.enter_context(nc.sbuf_tensor("negmx", [128, 2, 4], F32))
    sums4 = p3.enter_context(nc.sbuf_tensor("sums4", [128, 2, 4], F32))
    rsum4 = p3.enter_context(nc.sbuf_tensor("rsum4", [128, 2, 4], F32))
    h2st = p3.enter_context(nc.sbuf_tensor("h2st", [128, 16, GT], FP8))

    wqT_r = [Res(), Res()]

    def emit_wqT_load(g, w, eb):
        par = (g * 12 + w * 4 + eb) % 2
        ws = P.w_waits(wqT_r[par]) + [(sSYNC, prep_done[w]),
                                      (sPE, h_trans_done)]
        p.emit("sync", lambda e, o=wqT_sb[:, par, :, :], i=wqT3[w][eb]:
               e.dma_start(out=o, in_=i), inc=(sWQT[par], 16), waits=ws)
        P.note_write(wqT_r[par], sWQT[par])

    h_done_act = sACT.n
    cast_jobs = []
    for g_ in range(2):
        for fc_ in range(4):
            cast_jobs.append(lambda e, o=wgq[g_, fc_ * 512:(fc_ + 1) * 512, :],
                             i=wg_in[g_, fc_ * 512:(fc_ + 1) * 512, :]:
                             e.dma_start(out=o, in_=i))
    for ub_ in range(8):
        cast_jobs.append(lambda e, o=wlq[ub_ * 1024:(ub_ + 1) * 1024, :],
                         i=wl_in[ub_ * 1024:(ub_ + 1) * 1024, :]:
                         e.dma_start(out=o, in_=i))
    cast_next = [0]

    def emit_ffn_casts(n, gate):
        for _ in range(n):
            if cast_next[0] >= len(cast_jobs):
                return
            p.emit("gpsimd", cast_jobs[cast_next[0]], inc=(sFFNQ, 16),
                   waits=(gate if gate is not None else ()))
            gate = None
            cast_next[0] += 1

    wqT_r = [Res(), Res()]
    wqT_pre = {}
    qkv_r = Res()
    attn_tile_r = [Res() for _ in range(4)]
    sub_r = {n: [Res() for _ in range(2)] for n in
             ("sm", "pb", "pn", "pt", "vt", "nm", "su", "rs")}
    h2st_r = Res()

    for g in range(NG):
        gsl = slice(g * GT, (g + 1) * GT)
        for w in range(3):
            dst_sb = (q_sb, k_sb, v_sb)[w]
            for eb in range(4):
                par = (g * 12 + w * 4 + eb) % 2
                if (g, w, eb) in wqT_pre:
                    ldv = wqT_pre.pop((g, w, eb))
                else:
                    emit_wqT_load(g, w, eb)
                    ldv = (sWQT[par], sWQT[par].n)
                for ec in range(4):
                    head = eb * 4 + ec
                    bi = 4 + (head % 2)
                    bank = fbank[bi]
                    bws = P.w_waits(bank) + [ldv, (sACT, h_tile_act[g * 4 + 3])]
                    if w == 0 and eb == 0 and ec == 0:
                        bws += P.w_waits(qkv_r)
                    for fc in range(16):
                        lhsT = wqT_sb[:, par, fc, ec * 128:(ec + 1) * 128]
                        rhs = h_fm[:, fc, gsl]
                        p.emit("tensor", lambda e, o=FB[bi][:, :], l=lhsT, r=rhs,
                               s=(fc == 0), t=(fc == 15):
                               e.matmul(o, l, r, start=s, stop=t),
                               inc=((sPE, 1) if fc == 15 else None),
                               waits=(bws if fc == 0 else ()))
                    P.note_write(bank, sPE)
                    P.note_read(wqT_r[par], sPE)
                    dst = dst_sb[:, :, :, head]
                    srcv = FB[bi][:, :].rearrange("p (st t) -> p st t", t=8)
                    p.emit("scalar", lambda e, o=dst, i_=srcv:
                           e.activation(out=o, in_=i_, func=AF.Copy),
                           waits=P.r_waits(bank))
                    P.note_read(bank, sACT)
        qkv_done = sACT.n
        if g + 1 < NG:
            for eb in range(2):
                emit_wqT_load(g + 1, 0, eb)
                parn = ((g + 1) * 12 + eb) % 2
                wqT_pre[(g + 1, 0, eb)] = (sWQT[parn], sWQT[parn].n)
        emit_ffn_casts(4, [(sACT, qkv_done)])
        attn_asm_val = [0, 0, 0, 0]
        exp_val = {}

        def emit_s4(w):
            par = w % 2
            bank = fbank[par]
            ws = P.w_waits(bank) + [(sACT, qkv_done)]
            for k in range(4):
                st = 4 * w + k
                lhsT = q_sb[:, st, :, :].rearrange("p t h -> p (t h)")
                rhs = k_sb[:, st, :, :].rearrange("p t h -> p (t h)")
                p.emit("tensor", lambda e, o=FB[par][:, k * 128:(k + 1) * 128],
                       l=lhsT, r=rhs, s=(k == 0):
                       e.matmul(o, l, r, start=s, stop=True, skip_group_check=True),
                       inc=((sPE, 1) if k == 3 else None),
                       waits=(ws if k == 0 else ()))
            P.note_write(bank, sPE)
            P.note_read(qkv_r, sPE)

        def emit_sma4(w):
            par = w % 2
            bank = fbank[par]
            smb = fbank[4 + par]
            r_sm = sub_r["sm"][par]
            ws = P.w_waits(r_sm) + P.r_waits(bank) + P.w_waits(smb)
            p.emit("vector", lambda e, o=FB[4 + par][:, :], i0=FB[par][:, :]:
                   e.scalar_tensor_tensor(out=o, in0=i0, scalar=1.0,
                                          in1=mask_sb[:, :],
                                          op0=ALU.mult, op1=ALU.add), waits=ws)
            P.note_read(bank, sDVE)
            P.note_write(r_sm, sDVE)
            smb.writer = (sDVE, sDVE.n)
            smb.readers = []
            r_nm = sub_r["nm"][par]
            srcv = FB[4 + par][:, :].rearrange("p (s f) -> p s f", s=4)
            p.emit("vector", lambda e, o=negmx[:, par, :], i_=srcv:
                   e.reduce_max(out=o, in_=i_, axis=AX.X, negate=True),
                   waits=P.w_waits(r_nm))
            P.note_write(r_nm, sDVE)
            dv = sDVE.n
            r_pb = sub_r["pb"][par]
            r_su = sub_r["su"][par]
            ws2 = P.w_waits(r_pb) + P.w_waits(r_su) + [(sDVE, dv)]
            for k in range(4):
                p.emit("scalar", lambda e, k=k, o=pbf4[:, par, k * 128:(k + 1) * 128],
                       i_=FB[4 + par][:, k * 128:(k + 1) * 128],
                       b=negmx[:, par, k:k + 1], a=sums4[:, par, k:k + 1]:
                       e.activation(out=o, in_=i_, func=AF.Exp, bias=b, scale=1.0,
                                    accum_out=a), waits=(ws2 if k == 0 else ()))
            P.note_read(r_sm, sACT)
            P.note_read(r_nm, sACT)
            P.note_write(r_pb, sACT)
            P.note_write(r_su, sACT)
            P.note_read(smb, sACT)
            exp_val[w] = sACT.n

        def emit_rrs4(w):
            par = w % 2
            r_su = sub_r["su"][par]
            r_rs = sub_r["rs"][par]
            p.emit("vector", lambda e, o=rsum4[:, par, :], i_=sums4[:, par, :]:
                   e.reciprocal(out=o, in_=i_),
                   waits=P.w_waits(r_rs) + [(sACT, exp_val[w])])
            P.note_read(r_su, sDVE)
            P.note_write(r_rs, sDVE)
            r_pb = sub_r["pb"][par]
            for k in range(4):
                p.emit("vector", lambda e, k=k, o=pbf4[:, par, k * 128:(k + 1) * 128],
                       i_=pbf4[:, par, k * 128:(k + 1) * 128],
                       s=rsum4[:, par, k:k + 1]:
                       e.tensor_scalar(out=o, in0=i_, scalar1=s, scalar2=None,
                                       op0=ALU.mult))
            P.note_read(r_rs, sDVE)
            r_pn = sub_r["pn"][par]
            P.note_write(r_pn, sDVE)

        def emit_ptt4(w):
            par = w % 2
            bb = bbank[par]
            r_pn = sub_r["pn"][par]
            ws = P.w_waits(bb) + P.r_waits(r_pn)
            for k in range(4):
                p.emit("tensor", lambda e, k=k, o=BB[par][:, k * 128:(k + 1) * 128],
                       i_=pbf4[:, par, k * 128:(k + 1) * 128]:
                       e.transpose(o, i_, id16[:, :]), waits=(ws if k == 0 else ()))
            P.note_read(r_pn, sPE)
            P.note_read(sub_r["pb"][par], sPE)
            for k in range(4):
                st = 4 * w + k
                v8f = v_sb[:, st, :, :].rearrange("p t h -> p (t h)")
                p.emit("tensor", lambda e, k=k,
                       o=BB[par][:, 512 + k * 128:512 + (k + 1) * 128], i_=v8f:
                       e.transpose(o, i_, id16[:, :]),
                       inc=((sPE, 1) if k == 3 else None))
            P.note_write(bb, sPE)
            P.note_read(qkv_r, sPE)
            tv = sPE.n
            r_pt = sub_r["pt"][par]
            r_vt = sub_r["vt"][par]
            p.emit("scalar", lambda e, o=pt4[:, par, :], i_=BB[par][:, 0:512]:
                   e.activation(out=o, in_=i_, func=AF.Copy),
                   waits=P.w_waits(r_pt) + [(sPE, tv)])
            P.note_write(r_pt, sACT)
            p.emit("scalar", lambda e, o=vt4[:, par, :], i_=BB[par][:, 512:1024]:
                   e.activation(out=o, in_=i_, func=AF.Copy),
                   waits=P.w_waits(r_vt))
            P.note_write(r_vt, sACT)
            P.note_read(bb, sACT)

        def emit_av4(w):
            par = w % 2
            bank = fbank[2 + par]
            r_pt = sub_r["pt"][par]
            r_vt = sub_r["vt"][par]
            ws = P.w_waits(bank) + P.r_waits(r_pt) + P.r_waits(r_vt)
            for k in range(4):
                p.emit("tensor", lambda e, k=k,
                       o=FB[2 + par][:, k * 128:(k + 1) * 128],
                       l=vt4[:, par, k * 128:(k + 1) * 128],
                       r=pt4[:, par, k * 128:(k + 1) * 128], s=(k == 0):
                       e.matmul(o, l, r, start=s, stop=True, skip_group_check=True),
                       inc=((sPE, 1) if k == 3 else None),
                       waits=(ws if k == 0 else ()))
            P.note_write(bank, sPE)
            P.note_read(r_pt, sPE)
            P.note_read(r_vt, sPE)
            tt = (4 * w) // 16
            ws3 = P.r_waits(bank) + P.w_waits(attn_tile_r[tt]) + [(sDVE, modvec_done)]
            gat = ga_til[:, :, :].rearrange("p h (s t) -> p h s t", s=2)
            for j in range(2):
                st0 = 4 * w + 2 * j
                dst = attn_sb[:, :, st0 * 8:(st0 + 2) * 8].rearrange(
                    "p h (s t) -> p h s t", s=2)
                src0 = FB[2 + par][:, 2 * j * 128:(2 * j + 2) * 128].rearrange(
                    "p (s t h) -> p h s t", s=2, t=8)
                p.emit("vector", lambda e, o=dst, a=src0, g_=gat: e.tensor_tensor(
                       out=o, in0=a, in1=g_, op=ALU.mult),
                       waits=(ws3 if j == 0 else ()))
            P.note_read(bank, sDVE)
            attn_asm_val[tt] = sDVE.n

        def emit_res_tile(g, tt):
            gi = g * 4 + tt
            par = gi % 2
            ws = P.w_waits(x_t_r[par])
            ld = p.emit("sync", lambda e, o=x_t[:, par, :],
                        i_=x_in[gi * 128:(gi + 1) * 128, :]:
                        e.dma_start(out=o, in_=i_), inc=(sX[par], 16), waits=ws)
            P.note_write(x_t_r[par], sX[par])
            wx1 = P.w_waits(x1_t_r)
            for hq in range(4):
                bb = bbank[hq % 2]
                ws2 = P.w_waits(bb) + ([(sDVE, attn_asm_val[tt])] if hq == 0 else [])
                for k in range(4):
                    hc = hq * 4 + k
                    p.emit("tensor", lambda e, k=k,
                           o=BB[hq % 2][:, k * 128:(k + 1) * 128],
                           i_=attn_sb[:, hc, tt * 128:(tt + 1) * 128]:
                           e.transpose(o, i_, id16[:, :]),
                           inc=((sPE, 1) if k == 3 else None),
                           waits=(ws2 if k == 0 else ()))
                P.note_write(bb, sPE)
                wv3 = P.r_waits(bb) + [(sX[par], ld)] + (wx1 if hq == 0 else [])
                p.emit("vector", lambda e, hq=hq,
                       o=x1_t[:, hq * 512:(hq + 1) * 512],
                       i0=BB[hq % 2][:, 0:512],
                       i1=x_t[:, par, hq * 512:(hq + 1) * 512]:
                       e.tensor_tensor(out=o, in0=i0, in1=i1, op=ALU.add),
                       waits=wv3)
                P.note_read(bb, sDVE)
            attn_tile_r[tt].writer = None
            attn_tile_r[tt].readers = [(sPE, sPE.n)]
            P.note_read(x_t_r[par], sDVE)
            P.note_write(x1_t_r, sDVE)
            rv = sDVE.n
            p.emit("sync", lambda e, o=x1d[gi * 128:(gi + 1) * 128, :],
                   i_=x1_t[:, :]: e.dma_start(out=o, in_=i_),
                   inc=(sX1S[par], 16), waits=[(sDVE, rv)])
            P.note_read(x1_t_r, sX1S[par])
            for wv in P.w_waits(ln_r[par]):
                p.wait("vector", wv[0], wv[1])
            emit_ln_chain(par, x1_t[:, :])
            P.note_write(ln_r[par], sDVE)
            emit_norm(par, x1_t[:, :])
            P.note_read(x1_t_r, sDVE)
            nval = sDVE.n
            for fc in range(16):
                bb = bbank[fc % 2]
                ws2 = P.w_waits(bb) + [(sDVE, nval)]
                p.emit("tensor", lambda e, o=BB[fc % 2][:, 0:128],
                       i_=norm_t[:, fc * 128:(fc + 1) * 128]:
                       e.transpose(o, i_, id16[:, :]), inc=(sPE, 1), waits=ws2)
                P.note_write(bb, sPE)
                P.note_read(norm_r[0], sPE)
                wv2 = P.r_waits(bb) + \
                    (P.w_waits(h2st_r) if (fc == 0 and tt == 0) else [])
                p.emit("vector", lambda e, fc=fc,
                       o=h2st[:, fc, tt * 128:(tt + 1) * 128],
                       i_=BB[fc % 2][:, 0:128]: e.tensor_scalar(
                       out=o, in0=i_, scalar1=cb1[:, fc:fc + 1],
                       scalar2=mod_fm[:, 48 + fc:49 + fc],
                       op0=ALU.mult, op1=ALU.add), waits=wv2)
                P.note_read(bb, sDVE)

        NW = NST // 4
        for w in range(NW + 1):
            if w < NW:
                emit_s4(w)
                emit_sma4(w)
            if w >= 1:
                emit_rrs4(w - 1)
                emit_ptt4(w - 1)
                emit_av4(w - 1)
            if w >= 1 and ((w - 1) % 4) == 3:
                emit_res_tile(g, (w - 1) // 4)
        hv = sDVE.n
        p.emit("sync", lambda e, o=h2d[:, g, :, :], i_=h2st[:, :, :]:
               e.dma_start(out=o, in_=i_), inc=(sH2S, 16), waits=[(sDVE, hv)])
        h2st_r.writer = None
        h2st_r.readers = [(sH2S, sH2S.n)]
        qkv_r.writer = None
        qkv_r.readers = [(sDVE, sDVE.n), (sPE, sPE.n), (sACT, sACT.n)]
        emit_ffn_casts(2, [(sDVE, sDVE.n)])

    emit_ffn_casts(len(cast_jobs), None)
    ffn_cast_done = sFFNQ.n
    p.wait("sync", sX1S[0], sX1S[0].n)
    p.wait("sync", sX1S[1], sX1S[1].n)
    p.wait("sync", sH2S, sH2S.n)
    sSYNC.n += 1
    p.emit("sync", lambda e: e.sem_inc(sSYNC.h, 1))
    p3_done = sSYNC.n
    p3_end = dict(pe=sPE.n, dve=sDVE.n, act=sACT.n)
    p3.close()
    h_scope.close()

    p4 = ExitStack()
    h2_sb = p4.enter_context(nc.sbuf_tensor("h2_sb", [128, 2, 16, GT], FP8))
    act_sb = p4.enter_context(nc.sbuf_tensor("act_sb", [128, 64, GT], FP8))
    wg_sb = p4.enter_context(nc.sbuf_tensor("wg_sb", [128, 2, 2, 16, 512], FP8))
    wl_sb = p4.enter_context(nc.sbuf_tensor("wl_sb", [128, 2, 4, 512], FP8))
    gelu_sb = p4.enter_context(nc.sbuf_tensor("gelu_sb", [128, 2, GT], BF16))
    g1_sb = p4.enter_context(nc.sbuf_tensor("g1_sb", [128, 2, GT], BF16))
    ffn_fm = p4.enter_context(nc.sbuf_tensor("ffn_fm", [128, 16, GT], BF16))
    ffn_t = p4.enter_context(nc.sbuf_tensor("ffn_t", [128, 2, D], BF16))
    out_t = p4.enter_context(nc.sbuf_tensor("out_t", [128, 2, D], F32))

    h2_r = [Res(), Res()]
    act_r = Res()
    wg_r = [Res(), Res()]
    wl_r = [Res(), Res()]
    gelu_r = [Res(), Res()]
    g1_r = [Res(), Res()]
    ffn_fm_r = Res()
    ffn_t_r = [Res(), Res()]
    out_t_r = [Res(), Res()]

    p4_gate = [(sSYNC, p3_done), (sPE, p3_end["pe"]), (sDVE, p3_end["dve"]),
               (sACT, p3_end["act"]), (sFFNQ, ffn_cast_done)]

    def emit_h2_load(g):
        buf = g % 2
        ws = P.w_waits(h2_r[buf]) + (p4_gate if g == 0 else [])
        p.emit("sync", lambda e, o=h2_sb[:, buf, :, :], i_=h2d[:, g, :, :]:
               e.dma_start(out=o, in_=i_), inc=(sH2L[buf], 16), waits=ws)
        h2_r[buf].writer = (sH2L[buf], sH2L[buf].n)
        h2_r[buf].readers = []

    emit_h2_load(0)

    def emit_wg_load(ub):
        par = ub % 2
        wsw = P.w_waits(wg_r[par])
        for gate in range(2):
            for fq in range(4):
                src = wgq[gate, fq * 512:(fq + 1) * 512,
                          ub * 512:(ub + 1) * 512].rearrange(
                    "(fc p) u -> p fc u", p=128)
                p.emit("sync", lambda e,
                       o=wg_sb[:, par, gate, fq * 4:(fq + 1) * 4, :], i=src:
                       e.dma_start(out=o, in_=i), inc=(sWG[par], 16),
                       waits=(wsw if (gate == 0 and fq == 0) else ()))
        P.note_write(wg_r[par], sWG[par])
        return (sWG[par], sWG[par].n)

    wg_pre = {}
    for g in range(NG):
        buf = g % 2
        h2v = h2_r[buf].writer
        for ub in range(16):
            par = ub % 2
            if (g, ub) in wg_pre:
                wgv = wg_pre.pop((g, ub))
            else:
                wgv = emit_wg_load(ub)
            for uc in range(4):
                ucp = uc % 2
                b0 = fbank[ucp]
                b1 = fbank[2 + ucp]
                uu = ub * 4 + uc
                DR = mybir.MatmulPerfMode.DoubleRow
                ws0 = P.w_waits(b0) + [wgv, h2v]
                for f2 in range(8):
                    lhsT = wg_sb[:, par, 0, 2 * f2:2 * f2 + 2, uc * 128:(uc + 1) * 128]
                    rhs = h2_sb[:, buf, 2 * f2:2 * f2 + 2, :]
                    p.emit("tensor", lambda e, o=FB[ucp][:, :], l=lhsT, r=rhs,
                           s=(f2 == 0), t=(f2 == 7):
                           e.matmul(o, l, r, start=s, stop=t, perf_mode=DR),
                           inc=((sPE, 1) if f2 == 7 else None),
                           waits=(ws0 if f2 == 0 else ()))
                P.note_write(b0, sPE)
                ws1 = P.w_waits(b1)
                for f2 in range(8):
                    lhsT = wg_sb[:, par, 1, 2 * f2:2 * f2 + 2, uc * 128:(uc + 1) * 128]
                    rhs = h2_sb[:, buf, 2 * f2:2 * f2 + 2, :]
                    p.emit("tensor", lambda e, o=FB[2 + ucp][:, :], l=lhsT, r=rhs,
                           s=(f2 == 0), t=(f2 == 7):
                           e.matmul(o, l, r, start=s, stop=t, perf_mode=DR),
                           inc=((sPE, 1) if f2 == 7 else None),
                           waits=(ws1 if f2 == 0 else ()))
                P.note_write(b1, sPE)
                P.note_read(wg_r[par], sPE)
                P.note_read(h2_r[buf], sPE)
                p.emit("scalar", lambda e, o=gelu_sb[:, ucp, :], i_=FB[ucp][:, :]:
                       e.activation(out=o, in_=i_, func=AF.Gelu_apprx_tanh),
                       waits=P.r_waits(b0) + P.w_waits(gelu_r[ucp]))
                P.note_read(b0, sACT)
                P.note_write(gelu_r[ucp], sACT)
                p.emit("scalar", lambda e, o=g1_sb[:, ucp, :], i_=FB[2 + ucp][:, :]:
                       e.activation(out=o, in_=i_, func=AF.Copy),
                       waits=P.r_waits(b1) + P.w_waits(g1_r[ucp]))
                P.note_read(b1, sACT)
                P.note_write(g1_r[ucp], sACT)
                ev = sACT.n
                wsx = [(sACT, ev)] + (P.w_waits(act_r) if (ub == 0 and uc == 0) else [])
                p.emit("vector", lambda e, o=act_sb[:, uu, :], a=gelu_sb[:, ucp, :],
                       b=g1_sb[:, ucp, :]:
                       e.tensor_tensor(out=o, in0=a, in1=b, op=ALU.mult), waits=wsx)
                P.note_read(gelu_r[ucp], sDVE)
                P.note_read(g1_r[ucp], sDVE)
        act_done = sDVE.n
        act_r.writer = (sDVE, act_done)
        act_r.readers = []
        if g + 1 < NG:
            emit_h2_load(g + 1)
            for ub in range(2):
                wg_pre[(g + 1, ub)] = emit_wg_load(ub)
        for eq in range(4):
            banks = [fbank[i] for i in range(4)]
            for ub in range(16):
                par = ub % 2
                wsl = P.w_waits(wl_r[par])
                src = wlq[ub * 512:(ub + 1) * 512,
                          eq * 512:(eq + 1) * 512].rearrange("(uc p) e -> p uc e",
                                                             p=128)
                p.emit("sync", lambda e, o=wl_sb[:, par, :, :], i=src:
                       e.dma_start(out=o, in_=i), inc=(sWL[par], 16), waits=wsl)
                P.note_write(wl_r[par], sWL[par])
                wlv = (sWL[par], sWL[par].n)
                DR = mybir.MatmulPerfMode.DoubleRow
                for u2 in range(2):
                    uu = ub * 4 + 2 * u2
                    for ec in range(4):
                        bank = banks[ec]
                        ws2 = [wlv]
                        if ub == 0 and u2 == 0:
                            ws2 += P.w_waits(bank) + [(sDVE, act_done)]
                        lhsT = wl_sb[:, par, 2 * u2:2 * u2 + 2,
                                     ec * 128:(ec + 1) * 128]
                        rhs = act_sb[:, uu:uu + 2, :]
                        last = (u2 == 1 and ec == 3)
                        p.emit("tensor", lambda e, o=FB[ec][:, :], l=lhsT, r=rhs,
                               s=(ub == 0 and u2 == 0), t=(ub == 15 and u2 == 1):
                               e.matmul(o, l, r, start=s, stop=t, perf_mode=DR),
                               inc=((sPE, 1) if last else None), waits=ws2)
                P.note_read(wl_r[par], sPE)
            for ec in range(4):
                P.note_write(banks[ec], sPE)
            P.note_read(act_r, sPE)
            for ec in range(4):
                ecg = eq * 4 + ec
                ws3 = P.r_waits(banks[ec]) + \
                    (P.w_waits(ffn_fm_r) if (eq == 0 and ec == 0) else [])
                p.emit("scalar", lambda e, ecg=ecg, o=ffn_fm[:, ecg, :],
                       i_=FB[ec][:, :], s=gb_c[:, ecg:ecg + 1]:
                       e.activation(out=o, in_=i_, func=AF.Identity, scale=s),
                       waits=ws3)
                P.note_read(banks[ec], sACT)
        ffn_ev = sACT.n
        ffn_fm_r.writer = (sACT, ffn_ev)
        ffn_fm_r.readers = []
        for tt in range(4):
            gi = g * 4 + tt
            par = gi % 2
            ld = p.emit("sync", lambda e, o=x1_t[:, :],
                        i_=x1d[gi * 128:(gi + 1) * 128, :]:
                        e.dma_start(out=o, in_=i_), inc=(sX1L[par], 16),
                        waits=P.w_waits(x1_t_r))
            P.note_write(x1_t_r, sX1L[par])
            wf = P.w_waits(ffn_t_r[par])
            for hc in range(16):
                bb = bbank[hc % 2]
                ws2 = P.w_waits(bb) + ([(sACT, ffn_ev)] if hc == 0 else [])
                p.emit("tensor", lambda e, o=BB[hc % 2][:, 0:128],
                       i_=ffn_fm[:, hc, tt * 128:(tt + 1) * 128]:
                       e.transpose(o, i_, id16[:, :]), inc=(sPE, 1), waits=ws2)
                P.note_write(bb, sPE)
                p.emit("scalar", lambda e, hc=hc,
                       o=ffn_t[:, par, hc * 128:(hc + 1) * 128],
                       i_=BB[hc % 2][:, 0:128]:
                       e.activation(out=o, in_=i_, func=AF.Copy),
                       waits=P.r_waits(bb) + (wf if hc == 0 else []))
                P.note_read(bb, sACT)
            P.note_read(ffn_fm_r, sPE)
            ev = sACT.n
            ffn_t_r[par].writer = (sACT, ev)
            ffn_t_r[par].readers = []
            wsr = P.w_waits(out_t_r[par]) + [(sX1L[par], ld), (sACT, ev)]
            p.emit("vector", lambda e, o=out_t[:, par, :], a=x1_t[:, :],
                   b=ffn_t[:, par, :]:
                   e.tensor_tensor(out=o, in0=a, in1=b, op=ALU.add), waits=wsr)
            P.note_read(x1_t_r, sDVE)
            P.note_read(ffn_t_r[par], sDVE)
            P.note_write(out_t_r[par], sDVE)
            rv = sDVE.n
            p.emit("sync", lambda e, o=y_out[gi * 128:(gi + 1) * 128, :],
                   i_=out_t[:, par, :]: e.dma_start(out=o, in_=i_),
                   inc=(sOUT[par], 16), waits=[(sDVE, rv)])
            P.note_read(out_t_r[par], sOUT[par])

    p.ops["sync"].append(lambda e, _h=sOUT[0].h, _v=sOUT[0].n: e.wait_ge(_h, _v))
    p.ops["sync"].append(lambda e, _h=sOUT[1].h, _v=sOUT[1].n: e.wait_ge(_h, _v))

    p4.close()

    for (ctr, val) in p.all_waits:
        assert val <= ctr.n, f"wait beyond final count: {val} > {ctr.n}"

    with nc.Block() as blk:
        @blk.sync
        def _(e):
            for f in p.ops["sync"]:
                f(e)

        @blk.gpsimd
        def _(e):
            for f in p.ops["gpsimd"]:
                f(e)

        @blk.tensor
        def _(e):
            for f in p.ops["tensor"]:
                f(e)

        @blk.vector
        def _(e):
            for f in p.ops["vector"]:
                f(e)

        @blk.scalar
        def _(e):
            for f in p.ops["scalar"]:
                f(e)

    es.close()
    return nc


def _make_consts():
    bf = ml_dtypes.bfloat16
    id16 = np.eye(128, dtype=np.float32).astype(bf)
    id32 = np.eye(128, dtype=np.float32)
    mask = np.full((128, 128), -1e30, dtype=np.float32)
    for t in range(8):
        blk = np.where(np.tril(np.ones((16, 16), dtype=bool)), 0.0, -1e30)
        mask[t * 16:(t + 1) * 16, t * 16:(t + 1) * 16] = blk
    mask4 = np.ascontiguousarray(np.tile(mask, (1, 4)).astype(np.float32))
    return id16, id32, mask4


def kernel(L0_attn_input_seq_embs, L0_attn_Wq, L0_attn_Wk, L0_attn_Wv,
           L0_ffn_w_gating, L0_ffn_w_linear, L0_c, L0_wmod):
    x = np.ascontiguousarray(np.asarray(L0_attn_input_seq_embs, dtype=np.float32))
    wq = np.ascontiguousarray(np.asarray(L0_attn_Wq, dtype=np.float32))
    wk = np.ascontiguousarray(np.asarray(L0_attn_Wk, dtype=np.float32))
    wv = np.ascontiguousarray(np.asarray(L0_attn_Wv, dtype=np.float32))
    wg = np.ascontiguousarray(np.asarray(L0_ffn_w_gating, dtype=np.float32))
    wl = np.ascontiguousarray(np.asarray(L0_ffn_w_linear, dtype=np.float32))
    c = np.ascontiguousarray(np.asarray(L0_c, dtype=np.float32))
    wmod = np.ascontiguousarray(np.asarray(L0_wmod, dtype=np.float32))

    id16, id32, mask = _make_consts()
    nc = build_nc()

    in_maps = []
    for core in range(8):
        b = core // 4
        s = core % 4
        in_maps.append({
            "x": np.ascontiguousarray(x[b, s * TOK:(s + 1) * TOK, :]),
            "cvec": np.ascontiguousarray(c[b:b + 1, :]),
            "wq": wq, "wk": wk, "wv": wv,
            "wg": wg, "wl": wl, "wmod": wmod,
            "id16": id16, "id32": id32, "mask": mask,
        })
    res = run_bass_kernel_spmd(nc, in_maps, core_ids=list(range(8)))
    y = np.empty((2, 8192, D), dtype=np.float32)
    for core in range(8):
        b = core // 4
        s = core % 4
        y[b, s * TOK:(s + 1) * TOK, :] = res.results[core]["y"]
    return y
